# version 11
# baseline (speedup 1.0000x reference)
import numpy as np
import ml_dtypes
from contextlib import ExitStack

import concourse.bass as bass
import concourse.mybir as mybir
from concourse.bass_utils import run_bass_kernel_spmd
import concourse.bass_utils as _bu
from pathlib import Path as _Path


def _bvo_ldwopt(tmpdir, inp="bir.json", outp="file.neff", arch=None, *, dve_root=None):
    cmd = [
        _bu.get_walrus_driver(),
        "--pass",
        ",".join([
            "birverifier", "runtime_memory_reservation", "lower_act",
            "lower_dve", "lower_ap_offset", "codegen", "neff_packager",
        ]),
        "-i", inp,
        "--neff-output-filename", outp,
        "--enable-birsim=true",
        "--mem-mode=physical",
        "--policy=0",
        "--enable-ldw-opt=true",
        "--assign-static-dmas-to-sp=false",
        f"--dram-page-size={_bu.aot_getenv('NEURON_SCRATCHPAD_PAGE_SIZE', '256')}",
        f"--enable-neff-debug-info={'false' if _bu.aot_checkenv('CONCOURSE_SCRUB_NEFF_DEBUG_INFO') else 'true'}",
        "--jobs", "8",
        *_bu.get_walrus_args(
            _bu.get_bir_arch(tmpdir, inp) if arch is None else arch,
            tmpdir, dve_root=dve_root,
        ),
    ]
    result = _bu.run_command(cmd, cwd=tmpdir)
    if result is not None:
        (_Path(tmpdir) / "log.txt").write_text(result.stdout)
    return f"{tmpdir}/{outp}"


_bu.bir_verify_and_optimise = _bvo_ldwopt

F32 = mybir.dt.float32
BF16 = mybir.dt.bfloat16
AF = mybir.ActivationFunctionType
ALU = mybir.AluOpType
AX = mybir.AxisListType

SEQ, D, TILE = 8192, 1024, 1024
NT = SEQ // TILE
NBLK = TILE // 128
DB = D // 128
M, NE, H, HD = 2048, 8, 16, 64
MB = M // 128
NPAIR = 8
NG, GH = 4, 4
LN_EPS, ATT_EPS, ROUTER_EPS = 1e-5, 1e-6, 1e-8

ENGINES = ("tensor", "scalar", "vector", "gpsimd", "sync")
COMPUTE_ENGINES = ("tensor", "scalar", "vector")


class Op:
    __slots__ = ("eng", "fn", "deps", "dma_sem", "flagged", "value", "tag")

    def __init__(self, eng, fn, deps, dma_sem, tag):
        self.eng, self.fn, self.deps, self.dma_sem, self.tag = eng, fn, deps, dma_sem, tag
        self.flagged = dma_sem is not None
        self.value = None


class Sched:
    def __init__(self):
        self.ops = {e: [] for e in ENGINES}
        self.last = {e: None for e in ENGINES}

    def op(self, eng, fn, deps=(), dma_sem=None, tag=""):
        deps = [d for d in deps if d is not None]
        o = Op(eng, fn, deps, dma_sem, tag)
        for d in deps:
            d.flagged = True
        self.ops[eng].append(o)
        self.last[eng] = o
        return o

    def barrier(self, tag="bar", extra=()):
        snap = {e: self.last[e] for e in COMPUTE_ENGINES}
        outs = {}
        for e in COMPUTE_ENGINES:
            deps = [snap[o] for o in COMPUTE_ENGINES if snap[o] is not None]
            deps += [d for d in extra if d is not None]
            outs[e] = self.op(e, lambda _: None, deps=deps, tag=tag + "_" + e)
        return outs

    def finalize(self):
        counters = {}
        for eng in ENGINES:
            for o in self.ops[eng]:
                if not o.flagged:
                    continue
                sem = o.dma_sem if o.dma_sem else "s_" + eng
                step = 16 if o.dma_sem else 1
                counters[sem] = counters.get(sem, 0) + step
                o.value = (sem, counters[sem])
        self.sem_names = sorted(counters)
        self.final_counts = counters

    def emit(self, eng, eobj, sems):
        waited = {}
        for o in self.ops[eng]:
            need = {}
            for d in o.deps:
                sem, val = d.value
                if val > need.get(sem, 0):
                    need[sem] = val
            for sem, val in need.items():
                if waited.get(sem, 0) >= val:
                    continue
                waited[sem] = val
                eobj.wait_ge(sems[sem], val)
            inst = o.fn(eobj)
            if o.flagged:
                sem, _ = o.value
                if inst is None:
                    inst = eobj.nop()
                inst.then_inc(sems[sem], 16 if o.dma_sem else 1)


def build_program(debug_taps=(), n_tiles=NT, level=8):
    nc = bass.Bass("TRN2", target_bir_lowering=False)

    seq = n_tiles * TILE
    x_in = nc.dram_tensor("x", [seq, D], F32, kind="ExternalInput").ap()
    w_in = {}
    for nm, shp in (("wq", [D, D]), ("wk", [D, D]), ("wv", [D, D]), ("wp", [D, D]),
                    ("bp", [D]), ("wrouter", [D, NE]),
                    ("wgate", [NE, D, M]), ("wup", [NE, D, M]), ("wout", [NE, M, D]),
                    ("bgate", [NE, M]), ("bup", [NE, M]), ("bout", [NE, D])):
        w_in[nm] = nc.dram_tensor(nm, shp, F32, kind="ExternalInput").ap()
    ident_in = nc.dram_tensor("ident", [128, 128], BF16, kind="ExternalInput").ap()
    trimask_in = nc.dram_tensor("trimask", [128, 128], BF16, kind="ExternalInput").ap()
    y_out = nc.dram_tensor("y", [seq, D], F32, kind="ExternalOutput").ap()

    scr_gate = nc.dram_tensor("scr_gate", [NE, D, M], BF16, kind="Internal").ap()
    scr_up = nc.dram_tensor("scr_up", [NE, D, M], BF16, kind="Internal").ap()
    scr_out = nc.dram_tensor("scr_out", [NE, M, D], BF16, kind="Internal").ap()
    scr_attn = nc.dram_tensor("scr_attn", [4, D, D], BF16, kind="Internal").ap()

    taps = {}
    tap_shapes = {
        "h_tok": ([128, NBLK, D], BF16), "hT": ([128, DB, TILE], BF16),
        "w_half": ([128, NBLK, NE], F32), "x_mid": ([128, NBLK, D], F32),
        "phiq": ([128, DB, TILE], BF16), "phik": ([128, DB, TILE], BF16),
        "v_tm": ([128, NBLK, H, 65], BF16), "S_sb": ([128, H, 65], BF16),
        "yT_s": ([128, NPAIR, 128], BF16), "hid": ([128, MB, TILE], BF16),
    }
    for name in debug_taps:
        shp, dt = tap_shapes[name.split("@")[0]]
        taps[name] = nc.dram_tensor("tap_" + name.replace("@", "_"), shp, dt,
                                    kind="ExternalOutput").ap()

    es = ExitStack()

    ARENA_PP = 204 * 1024
    arena = nc.alloc_sbuf_tensor("arena", [128, ARENA_PP], mybir.dt.uint8)
    arena_base = nc.lookup_mloc(arena).addr
    offset_cursor = [0]

    def at(name, shape, dtype, rel=None, align=64):
        if rel is None:
            offset_cursor[0] = (offset_cursor[0] + align - 1) // align * align
            rel = offset_cursor[0]
            sz = int(np.prod(shape[1:])) * mybir.dt.size(dtype)
            offset_cursor[0] += sz
        assert rel + int(np.prod(shape[1:])) * mybir.dt.size(dtype) <= ARENA_PP, name
        h = nc.alloc_sbuf_tensor_at(name, list(shape), dtype, offset=arena_base + rel)
        return h, rel

    def at_(name, shape, dtype, rel=None, align=64):
        return at(name, shape, dtype, rel, align)[0]

    x_sb = at_("x_sb", [128, NBLK, D], F32)
    sq_col = at_("sq_col", [128, NBLK], F32)
    sum_col = at_("sum_col", [128, NBLK], F32)
    mcol = at_("mcol", [128, NBLK], F32)
    v1col = at_("v1col", [128, NBLK], F32)
    m2col = at_("m2col", [128, NBLK], F32)
    varcol = at_("varcol", [128, NBLK], F32)
    rstd = at_("rstd", [128, NBLK], F32)
    nmr = at_("nmr", [128, NBLK], F32)
    zz_col = at_("zz_col", [128, NPAIR], F32)
    z_f32 = at_("z_f32", [128, NPAIR], F32)
    lg_tok = at_("lg_tok", [128, NBLK, NE], F32)
    el_t = at_("el_t", [128, NBLK, NE], F32)
    wr_t = at_("wr_t", [128, NBLK, NE], F32)
    w_half = at_("w_half", [128, NBLK, NE], F32)
    pm_t = at_("pm_t", [128, NBLK, NE], F32)
    keep_t = at_("keep_t", [128, NBLK, NE], F32)
    mx_r = at_("mx_r", [128, NBLK], F32)
    s8_r = at_("s8_r", [128, NBLK], F32)
    m1_r = at_("m1_r", [128, NBLK], F32)
    m2_r = at_("m2_r", [128, NBLK], F32)
    s2_r = at_("s2_r", [128, NBLK], F32)
    den_r = at_("den_r", [128, NBLK], F32)
    rden_r = at_("rden_r", [128, NBLK], F32)
    negmx = at_("negmx", [128, NBLK], F32)
    rd_s = at_("rd_s", [128, GH], F32)
    eps_c = at_("eps_c", [128, 1], F32)
    neghalf_c = at_("neghalf_c", [128, 1], F32)
    bgate_col = at_("bgate_col", [128, MB, NE], F32)
    bup_col = at_("bup_col", [128, MB, NE], F32)

    h_tok, o_htok = at("h_tok", [128, NBLK, D], BF16)
    v_tm = at_("v_tm", [128, NBLK, H, 65], BF16, rel=o_htok)
    offset_cursor[0] = max(offset_cursor[0], o_htok + NBLK * H * 65 * 2)
    hT = at_("hT", [128, DB, TILE], BF16)
    hid, o_hid = at("hid", [128, MB, TILE], BF16)
    phiq = at_("phiq", [128, DB, TILE], BF16, rel=o_hid)
    phik = at_("phik", [128, DB, TILE], BF16, rel=o_hid + DB * TILE * 2)
    wslotA, o_wA = at("wslotA", [128, 16384], BF16)
    wslotB, o_wB = at("wslotB", [128, 16384], BF16)
    bg_stage = at_("bg_stage", [NE, M], BF16, rel=o_wA)
    bu_stage = at_("bu_stage", [NE, M], BF16, rel=o_wA + M * 2)
    gtmp, o_gtmp = at("gtmp", [128, 4, TILE], BF16)
    xsq = at_("xsq", [128, TILE], F32, rel=o_gtmp)
    e_t = at_("e_t", [128, TILE], F32, rel=o_gtmp)
    o_utmp = o_gtmp + 2 * TILE * 2
    tq = at_("tq", [128, TILE], F32, rel=o_utmp)
    lgT_s = at_("lgT_s", [NE, TILE], BF16, rel=o_utmp)
    a_m = at_("a_m", [128, 2, GH, 128], BF16)
    phik_tm = at_("phik_tm", [128, NPAIR, 128], BF16)
    phik_z = at_("phik_z", [128, GH, 128], BF16)
    y_tok = at_("y_tok", [128, 2, 2, 128], BF16)
    yT_s = at_("yT_s", [128, NPAIR, 128], BF16)
    S_sb = at_("S_sb", [128, H, 65], BF16)
    bout_mat = at_("bout_mat", [NE, D], BF16)
    w_bf = at_("w_bf", [128, NBLK, NE], BF16)
    wT_s = at_("wT_s", [NE, NBLK, 128], BF16)
    bp_row = at_("bp_row", [1, D], BF16)
    wrouter_sb = at_("wrouter_sb", [128, DB, NE], BF16)
    trimask = at_("trimask", [128, 128], BF16)
    ident = at_("ident", [128, 128], BF16)
    ones_r = at_("ones_r", [1, 128], BF16)

    assert offset_cursor[0] <= ARENA_PP, offset_cursor[0]

    psS = es.enter_context(nc.psum_tensor("psS", [128, NPAIR, HD], F32))
    PF0 = es.enter_context(nc.psum_tensor("PF0", [128, 1024], F32))
    PF1 = es.enter_context(nc.psum_tensor("PF1", [128, 1024], F32))
    PB0 = es.enter_context(nc.psum_tensor("PB0", [128, NBLK, 128], BF16))
    PB1 = es.enter_context(nc.psum_tensor("PB1", [128, NBLK, 128], BF16))
    PF2 = es.enter_context(nc.psum_tensor("PF2", [128, GH, 128], F32))

    S = Sched()

    def slot3(slot, nb, width, ofs=0):
        return bass.AP(slot, ofs, [[16384, 128], [width, nb], [1, width]])

    def MM(out_ap, lhsT_ap, rhs_ap, start, stop):
        return lambda t: t.matmul(out_ap, lhsT_ap, rhs_ap, start=start, stop=stop)

    def TR(out_ap, in_ap, id_ap):
        return lambda t: t.transpose(out_ap, in_ap, id_ap)

    setup_casts = []
    for e in range(NE):
        for src, dst, nm in ((w_in["wgate"][e], scr_gate[e], "g"),
                             (w_in["wup"][e], scr_up[e], "u"),
                             (w_in["wout"][e], scr_out[e], "o")):
            setup_casts.append(S.op("gpsimd", (lambda s_, d_: lambda g: g.dma_start(d_, s_))(src, dst),
                                    dma_sem="q_set", tag=f"cast_{nm}{e}"))

    ld_tri = S.op("gpsimd", lambda g: g.dma_start(trimask[:], trimask_in), dma_sem="q_set", tag="ld_tri")
    ld_id = S.op("gpsimd", lambda g: g.dma_start(ident[:], ident_in), dma_sem="q_set", tag="ld_id")
    ld_bout = S.op("gpsimd", lambda g: g.dma_start(bout_mat[:], w_in["bout"]), dma_sem="q_set", tag="ld_bout")
    ld_bp = S.op("gpsimd", lambda g: g.dma_start(bp_row[0:1, :], w_in["bp"][None, :]), dma_sem="q_set", tag="ld_bp")
    ld_wrt = S.op("gpsimd", lambda g: g.dma_start(wrouter_sb[:], w_in["wrouter"].rearrange("(b p) e -> p b e", p=128)), dma_sem="q_set", tag="ld_wrt")
    ld_bg = S.op("gpsimd", lambda g: g.dma_start(bg_stage[:], w_in["bgate"]), dma_sem="q_set", tag="ld_bg")
    ld_bu = S.op("gpsimd", lambda g: g.dma_start(bu_stage[:], w_in["bup"]), dma_sem="q_set", tag="ld_bu")
    ms_ones = S.op("gpsimd", lambda g: g.memset(ones_r[:], 1.0), tag="ms_ones")
    setup_fence = S.op("gpsimd", lambda g: None,
                       deps=[setup_casts[-1], ld_tri, ld_id, ld_bout, ld_bp, ld_wrt,
                             ld_bg, ld_bu, ms_ones], tag="setup_fence")
    ms_S = S.op("vector", lambda v: v.memset(S_sb[:], 0.0), tag="ms_S")
    ms_kz = S.op("vector", lambda v: v.memset(phik_z[:], 0.0), tag="ms_kz")
    ms_eps = S.op("vector", lambda v: v.memset(eps_c[:], LN_EPS), tag="ms_eps")
    ms_nh = S.op("vector", lambda v: v.memset(neghalf_c[:], -0.5), tag="ms_nh")
    ms_Z = S.op("vector", lambda v: v.memset(z_f32[:], 0.0), tag="ms_Z")

    prev = None
    for stage, col, nm in ((bg_stage, bgate_col, "g"), (bu_stage, bup_col, "u")):
        for mb in range(MB):
            tr = S.op("tensor",
                      (lambda st, m: lambda t: t.transpose(PB1[:, 0, 0:8], st[0:NE, m * 128:(m + 1) * 128], ident[0:NE, 0:NE]))(stage, mb),
                      deps=[setup_fence, prev], tag=f"btr_{nm}{mb}")
            prev = S.op("vector",
                        (lambda m, c: lambda v: v.tensor_copy(c[:, m, :], PB1[:, 0, 0:8]))(mb, col),
                        deps=[tr], tag=f"bev_{nm}{mb}")
    bias_done = prev

    srcs = (w_in["wq"], w_in["wk"], w_in["wv"], w_in["wp"])
    aw_store = []
    prev_store = bias_done
    for mi in range(4):
        ld = S.op("gpsimd",
                  (lambda ws: lambda g: g.dma_start(slot3(wslotA, DB, D), ws.rearrange("(b p) d -> p b d", p=128)))(srcs[mi]),
                  deps=[prev_store], dma_sem="q_aw", tag=f"aw_ld{mi}")
        prev_ev = None
        evs = []
        for fill in range(8):
            trs = []
            for k in range(8):
                idx = fill * 8 + k
                ib, db = idx // DB, idx % DB
                trs.append(S.op("tensor",
                                (lambda i_, d_, kk: lambda t: t.transpose(PB0[:, kk, :], wslotA[:, i_ * D + d_ * 128:i_ * D + (d_ + 1) * 128], ident[:]))(ib, db, k),
                                deps=[ld, setup_fence, prev_ev] if k == 0 else [], tag=f"awtr{mi}_{idx}"))
            def mk_ev(fill):
                def f(v):
                    last = None
                    for k in range(8):
                        idx = fill * 8 + k
                        ib, db = idx // DB, idx % DB
                        last = v.tensor_copy(wslotB[:, db * D + ib * 128: db * D + (ib + 1) * 128], PB0[:, k, :])
                    return last
                return f
            prev_ev = S.op("vector", mk_ev(fill), deps=trs, tag=f"awev{mi}_{fill}")
            evs.append(prev_ev)
        st = S.op("sync",
                  (lambda m_: lambda sy: sy.dma_start(scr_attn[m_].rearrange("(b p) d -> p b d", p=128), slot3(wslotB, DB, D)))(mi),
                  deps=evs, dma_sem="q_scr", tag=f"awst{mi}")
        prev_store = st
        aw_store.append(st)

    setup_done = [aw_store[-1], setup_fence, ms_S, ms_kz, ms_Z, ms_eps, ms_nh, bias_done]
    st_pb_init = prev_ev

    HS = 8192
    hs_tensor = [wslotA, wslotA, wslotB, wslotB]
    hs_off = [0, HS, 0, HS]
    hs_sems = ["q_w0", "q_w1", "q_w2", "q_w3"]
    hs_last_use = [None, None, None, None]
    hs_seq = [0]

    def hs_view(k, pattern):
        return bass.AP(hs_tensor[k], hs_off[k], [[16384, 128]] + pattern)

    def hs_col(k, c0, n):
        return bass.AP(hs_tensor[k], hs_off[k] + c0, [[16384, 128], [1, n]])

    def load_half(kind, e=0, h=0):
        k = hs_seq[0] % 4
        hs_seq[0] += 1
        deps = [hs_last_use[k]] if hs_last_use[k] is not None else list(setup_done)
        if kind in ("gate", "up"):
            scr = scr_gate if kind == "gate" else scr_up
            src = scr[e].rearrange("(b p) m -> p b m", p=128)[:, :, h * 1024:(h + 1) * 1024]
            dst = hs_view(k, [[1024, DB], [1, 1024]])
        elif kind == "out":
            src = scr_out[e].rearrange("(b p) d -> p b d", p=128)[:, :, h * 512:(h + 1) * 512]
            dst = hs_view(k, [[512, MB], [1, 512]])
        elif kind == "attn":
            src = scr_attn[e].rearrange("(b p) d -> p b d", p=128)
            dst = hs_view(k, [[1024, DB], [1, 1024]])
        fn = (lambda d_, s_: lambda g: g.dma_start(d_, s_))(dst, src)
        op = S.op("sync", fn, deps=deps, dma_sem=hs_sems[k], tag=f"wld_{kind}{e}_{h}")
        return op, k

    def mark_half_use(k, op):
        hs_last_use[k] = op

    st = {
        "scopy": None, "zcopy": None, "first_supd": True,
        "x_store": None, "last_tap": None, "first_supd_par": [True, True],
        "pb_free": None, "pf2_free": None,
    }
    st["pb_free"] = st_pb_init

    def tap(name, src_ap, deps):
        if name in taps:
            op = S.op("sync", (lambda d_, s_: lambda sy: sy.dma_start(d_, s_))(taps[name], src_ap),
                      deps=deps, dma_sem="q_tap", tag="tap_" + name)
            st["last_tap"] = op
            return op
        return None

    def ln_phase(t, sub, x_deps):
        tap_dep = st["last_tap"]
        per_tb = list(x_deps) if len(x_deps) == NBLK else [x_deps[-1]] * NBLK
        sq_ops, red_ops = [], []
        for tb in range(NBLK):
            sq_ops.append(S.op("scalar",
                               (lambda b: lambda s: s.activation(xsq[:, :], x_sb[:, b, :], AF.Square, accum_out=sq_col[:, b:b + 1]))(tb),
                               deps=[per_tb[tb], tap_dep] + sq_ops[-1:], tag=f"t{t}s{sub}_sq{tb}"))
            red_ops.append(S.op("vector",
                                (lambda b: lambda v: v.reduce_sum(sum_col[:, b:b + 1], x_sb[:, b, :], axis=AX.X))(tb),
                                deps=[per_tb[tb], tap_dep], tag=f"t{t}s{sub}_red{tb}"))
        o_m = [S.op("vector", (lambda b: lambda v: v.tensor_scalar_mul(mcol[:, b:b + 1], sum_col[:, b:b + 1], 1.0 / D))(tb),
                    deps=[red_ops[tb]], tag=f"t{t}s{sub}_m{tb}") for tb in range(NBLK)]
        o_v1 = [S.op("vector", (lambda b: lambda v: v.tensor_scalar_mul(v1col[:, b:b + 1], sq_col[:, b:b + 1], 1.0 / D))(tb),
                     deps=[sq_ops[tb]], tag=f"t{t}s{sub}_v1{tb}") for tb in range(NBLK)]
        o_m2 = [S.op("vector", (lambda b: lambda v: v.tensor_mul(m2col[:, b:b + 1], mcol[:, b:b + 1], mcol[:, b:b + 1]))(tb),
                     deps=[o_m[tb]], tag=f"t{t}s{sub}_m2{tb}") for tb in range(NBLK)]
        o_var = [S.op("vector", (lambda b: lambda v: v.tensor_sub(varcol[:, b:b + 1], v1col[:, b:b + 1], m2col[:, b:b + 1]))(tb),
                      deps=[o_v1[tb], o_m2[tb]], tag=f"t{t}s{sub}_var{tb}") for tb in range(NBLK)]
        o_ln = [S.op("scalar", (lambda b: lambda s: s.activation(varcol[:, b:b + 1], varcol[:, b:b + 1], AF.Ln, bias=eps_c[:, 0:1]))(tb),
                     deps=[o_var[tb]], tag=f"t{t}s{sub}_ln{tb}") for tb in range(NBLK)]
        o_ex = [S.op("scalar", (lambda b: lambda s: s.activation(rstd[:, b:b + 1], varcol[:, b:b + 1], AF.Exp, scale=neghalf_c[:, 0:1]))(tb),
                     deps=[o_ln[tb]], tag=f"t{t}s{sub}_ex{tb}") for tb in range(NBLK)]
        o_nm = [S.op("vector", (lambda b: lambda v: v.scalar_tensor_tensor(nmr[:, b:b + 1], mcol[:, b:b + 1], -1.0, rstd[:, b:b + 1], ALU.mult, ALU.mult))(tb),
                     deps=[o_ex[tb]], tag=f"t{t}s{sub}_nm{tb}") for tb in range(NBLK)]
        app_ops = []
        for tb in range(NBLK):
            app_ops.append(S.op("scalar",
                                (lambda b: lambda s: s.activation(h_tok[:, b, :], x_sb[:, b, :], AF.Identity, bias=nmr[:, b:b + 1], scale=rstd[:, b:b + 1]))(tb),
                                deps=[o_nm[tb], o_ex[tb]], tag=f"t{t}s{sub}_app{tb}"))
        ev_ops = []
        idx = 0
        for fill in range(16):
            ps = PB0 if fill % 2 == 0 else PB1
            trs = []
            for k in range(4):
                tb, db = idx // DB, idx % DB
                dep = [app_ops[tb]]
                if fill < 2 and k == 0:
                    dep.append(st["pb_free"])
                if len(ev_ops) >= 2:
                    dep.append(ev_ops[-2])
                trs.append(S.op("tensor",
                                (lambda b, d_, kk, p_: lambda te: te.transpose(p_[:, kk, :], h_tok[:, b, d_ * 128:(d_ + 1) * 128], ident[:]))(tb, db, k, ps),
                                deps=dep, tag=f"t{t}s{sub}_htr{idx}"))
                idx += 1
            def mk_ev(fill, ps):
                def f(v):
                    last = None
                    for k in range(4):
                        i2 = fill * 4 + k
                        tb, db = i2 // DB, i2 % DB
                        last = v.tensor_copy(hT[:, db, tb * 128:(tb + 1) * 128], ps[:, k, :])
                    return last
                return f
            ev_ops.append(S.op("vector", mk_ev(fill, ps), deps=trs, tag=f"t{t}s{sub}_hev{fill}"))
        st["pb_free"] = ev_ops[-1]
        tap(f"h_tok@{t}_{sub}", h_tok[:], app_ops)
        tap(f"hT@{t}_{sub}", hT[:], ev_ops)
        return [ev_ops[7], ev_ops[15]], app_ops

    def router_phase(t, sub, hT_deps, app_ops, x_prev):
        b0 = S.barrier(f"t{t}s{sub}_rtbar", extra=[st["last_tap"]])
        lg_mms = []
        for h2 in range(2):
            for db in range(DB):
                dep = []
                if db == 0:
                    dep = [hT_deps[h2]] + ([setup_fence] if h2 == 0 else [])
                lg_mms.append(S.op("tensor",
                                   MM(PF0[0:NE, h2 * 512:(h2 + 1) * 512], wrouter_sb[:, db, :], hT[:, db, h2 * 512:(h2 + 1) * 512], db == 0, db == DB - 1),
                                   deps=dep, tag=f"t{t}s{sub}_lgmm"))
        ev1 = S.op("vector", lambda v: v.tensor_copy(lgT_s[:, :], PF0[0:NE, :]), deps=[lg_mms[-1]], tag=f"t{t}s{sub}_lgev")
        trs = []
        for tb in range(NBLK):
            trs.append(S.op("tensor",
                            (lambda b: lambda te: te.transpose(PB0[:, b, 0:8], lgT_s[0:NE, b * 128:(b + 1) * 128], ident[0:NE, 0:NE]))(tb),
                            deps=[ev1] if tb == 0 else [], tag=f"t{t}s{sub}_lgtr{tb}"))
        lg_src = bass.AP(PB0, 0, [[NBLK * 128, 128], [128, NBLK], [1, NE]])
        ev2 = S.op("vector", lambda v: v.tensor_copy(lg_tok[:], lg_src), deps=trs, tag=f"t{t}s{sub}_lgev2")

        def batch(opname, mk, deps_of):
            return [S.op("vector", mk(b), deps=deps_of(b), tag=f"t{t}s{sub}_{opname}{b}") for b in range(NBLK)]

        o_mx = batch("mx", lambda b: lambda v: v.reduce_max(mx_r[:, b:b + 1], lg_tok[:, b, :], axis=AX.X), lambda b: [ev2])
        o_ng = batch("ng", lambda b: lambda v: v.tensor_scalar_mul(negmx[:, b:b + 1], mx_r[:, b:b + 1], -1.0), lambda b: [o_mx[b]])
        o_el = [S.op("scalar",
                     (lambda b: lambda s: s.activation(el_t[:, b, :], lg_tok[:, b, :], AF.Exp, bias=negmx[:, b:b + 1]))(tb2),
                     deps=[o_ng[tb2], ev2], tag=f"t{t}s{sub}_el{tb2}") for tb2 in range(NBLK)]
        o_s8 = batch("s8", lambda b: lambda v: v.reduce_sum(s8_r[:, b:b + 1], el_t[:, b, :], axis=AX.X), lambda b: [o_el[b]])
        o_m1 = batch("m1", lambda b: lambda v: v.reduce_max(m1_r[:, b:b + 1], el_t[:, b, :], axis=AX.X), lambda b: [o_el[b]])
        o_pmm = batch("pmm", lambda b: lambda v: v.tensor_scalar(pm_t[:, b, :], el_t[:, b, :], m1_r[:, b:b + 1], None, ALU.is_lt), lambda b: [o_m1[b]])
        o_pm = batch("pm", lambda b: lambda v: v.tensor_mul(pm_t[:, b, :], pm_t[:, b, :], el_t[:, b, :]), lambda b: [o_pmm[b]])
        o_m2 = batch("m2", lambda b: lambda v: v.reduce_max(m2_r[:, b:b + 1], pm_t[:, b, :], axis=AX.X), lambda b: [o_pm[b]])
        o_kp = batch("kp", lambda b: lambda v: v.tensor_scalar(keep_t[:, b, :], el_t[:, b, :], m2_r[:, b:b + 1], None, ALU.is_ge), lambda b: [o_m2[b]])
        o_wr = batch("wr", lambda b: lambda v: v.tensor_mul(wr_t[:, b, :], keep_t[:, b, :], el_t[:, b, :]), lambda b: [o_kp[b]])
        o_s2 = batch("s2", lambda b: lambda v: v.reduce_sum(s2_r[:, b:b + 1], wr_t[:, b, :], axis=AX.X), lambda b: [o_wr[b]])
        o_dn = batch("dn", lambda b: lambda v: v.scalar_tensor_tensor(den_r[:, b:b + 1], s8_r[:, b:b + 1], ROUTER_EPS, s2_r[:, b:b + 1], ALU.mult, ALU.add), lambda b: [o_s2[b], o_s8[b]])
        o_rd = batch("rd", lambda b: lambda v: v.reciprocal(rden_r[:, b:b + 1], den_r[:, b:b + 1]), lambda b: [o_dn[b]])
        o_wh = batch("wh", lambda b: lambda v: v.tensor_scalar(w_half[:, b, :], wr_t[:, b, :], rden_r[:, b:b + 1], 0.5, ALU.mult, ALU.mult), lambda b: [o_rd[b]])
        tap(f"w_half@{t}_{sub}", w_half[:], o_wh)
        wbf = S.op("vector", lambda v: v.tensor_copy(w_bf[:], w_half[:]), deps=o_wh, tag=f"t{t}s{sub}_wbf")
        trs2 = []
        for tb in range(NBLK):
            dep = [wbf] if tb > 0 else [wbf, ev2]
            trs2.append(S.op("tensor",
                             (lambda b: lambda te: te.transpose(PB0[0:NE, b, :], w_bf[:, b, :], ident[:]))(tb),
                             deps=dep, tag=f"t{t}s{sub}_wtr{tb}"))
        ev3 = S.op("vector", lambda v: v.tensor_copy(wT_s[:], PB0[0:NE, :, :]), deps=trs2, tag=f"t{t}s{sub}_wTev")
        st["pb_free"] = ev3
        return o_wh, ev1, ev3

    def ffn_phase(t, sub, hT_deps, w_ops, ev1, ev3, app_ops, x_prev):
        pfr = [ev1, None]
        stt_ops = []
        last_stt_tb = list(x_prev)
        hid_hist = [None] * MB
        prev_hid = [None] * MB
        last_tb_final = [None] * NBLK
        PF2_flat = bass.AP(PF2, 0, [[512, 128], [1, 512]])
        bias_state = {"prev": st.get("pf2_free"), "i": 0}

        def emit_bias_piece():
            i = bias_state["i"]
            if i >= 16:
                return
            bias_state["i"] += 1
            tb, h2 = i // 2, i % 2
            bmm = S.op("tensor",
                       (lambda b, h_: lambda te: te.matmul(PF2_flat, wT_s[0:NE, b, :], bout_mat[0:NE, h_ * 512:(h_ + 1) * 512], start=True, stop=True))(tb, h2),
                       deps=[ev3, setup_fence, bias_state["prev"]], tag=f"t{t}s{sub}_bmm{tb}_{h2}")
            badd = S.op("vector",
                        (lambda b, h_: lambda v: v.tensor_add(x_sb[:, b, h_ * 512:(h_ + 1) * 512], x_sb[:, b, h_ * 512:(h_ + 1) * 512], PF2_flat))(tb, h2),
                        deps=[bmm, app_ops[tb], x_prev[tb], last_stt_tb[tb]], tag=f"t{t}s{sub}_badd{tb}_{h2}")
            bias_state["prev"] = badd
            last_stt_tb[tb] = badd
            st["pf2_free"] = badd

        for e in range(NE):
            ld_gA, k_gA = load_half("gate", e, 0)
            ld_gB, k_gB = load_half("gate", e, 1)
            ld_uA, k_uA = load_half("up", e, 0)
            ld_uB, k_uB = load_half("up", e, 1)
            ld_o = [None, None]
            k_o = [None, None]
            ev_g = [None] * MB
            hid_hist = [None] * MB
            for mb in range(MB):
                k_g = k_gA if mb < 8 else k_gB
                k_u = k_uA if mb < 8 else k_uB
                ld_g = ld_gA if mb < 8 else ld_gB
                ld_u = ld_uA if mb < 8 else ld_uB
                mcol0 = (mb % 8) * 128
                mms = []
                for h2 in range(2):
                    for db in range(DB):
                        dep = []
                        if h2 == 0 and db == 0:
                            dep = [hT_deps[0], ld_g]
                            dep.append(ev_g[mb - 1] if mb >= 1 else pfr[0])
                        elif h2 == 1 and db == 0 and mb == 0:
                            dep = [hT_deps[1]]
                        mms.append(S.op("tensor",
                                        MM(PF0[:, h2 * 512:(h2 + 1) * 512], hs_col(k_g, db * 1024 + mcol0, 128), hT[:, db, h2 * 512:(h2 + 1) * 512], db == 0, db == DB - 1),
                                        deps=dep, tag=f"t{t}s{sub}e{e}g{mb}"))
                g_last = mms[-1]
                mark_half_use(k_g, g_last)
                if e == 0 and mb >= 8:
                    emit_bias_piece()
                    emit_bias_piece()
                if mb == 7:
                    ld_o[0], k_o[0] = load_half("out", e, 0)
                dep_ge = [g_last, ev_g[mb - 2] if mb >= 2 else None,
                          hid_hist[mb - 4] if mb >= 4 else prev_hid[MB - 4 + mb]]
                ev_g[mb] = S.op("scalar",
                                (lambda m, ee: lambda s: s.activation(gtmp[:, m % 4, :], PF0[:], AF.Silu, bias=bgate_col[:, m, ee:ee + 1]))(mb, e),
                                deps=dep_ge, tag=f"t{t}s{sub}e{e}gev{mb}")
                mms_u = []
                for h2 in range(2):
                    for db in range(DB):
                        dep = []
                        if h2 == 0 and db == 0:
                            dep = [ld_u, hT_deps[0]]
                            dep.append(hid_hist[mb - 1] if mb >= 1 else pfr[1])
                        elif h2 == 1 and db == 0 and mb == 0:
                            dep = [hT_deps[1]]
                        mms_u.append(S.op("tensor",
                                          MM(PF1[:, h2 * 512:(h2 + 1) * 512], hs_col(k_u, db * 1024 + mcol0, 128), hT[:, db, h2 * 512:(h2 + 1) * 512], db == 0, db == DB - 1),
                                          deps=dep, tag=f"t{t}s{sub}e{e}u{mb}"))
                u_last = mms_u[-1]
                mark_half_use(k_u, u_last)
                if mb == 15:
                    ld_o[1], k_o[1] = load_half("out", e, 1)
                hid_hist[mb] = S.op("vector",
                                    (lambda m, ee: lambda v: v.scalar_tensor_tensor(hid[:, m, :], PF1[:], bup_col[:, m, ee:ee + 1], gtmp[:, m % 4, :], ALU.add, ALU.mult))(mb, e),
                                    deps=[u_last, ev_g[mb]], tag=f"t{t}s{sub}e{e}h{mb}")
            quarters = [PF0[:, 0:512], PF0[:, 512:1024], PF1[:, 0:512], PF1[:, 512:1024]]
            q_free = [ev_g[MB - 1], ev_g[MB - 1], hid_hist[MB - 1], hid_hist[MB - 1]]
            down_last = [None, None]
            for h2 in range(2):
                for tb in range(NBLK):
                    j = (h2 * NBLK + tb) % 4
                    mms_o = []
                    first_q = (h2 == 0 and tb < 4)
                    for mb in range(MB):
                        dep = []
                        if mb == 0:
                            dep = [ld_o[h2], q_free[j]] + (hid_hist if first_q else [])
                        mms_o.append(S.op("tensor",
                                          MM(quarters[j], hid[:, mb, tb * 128:(tb + 1) * 128], hs_col(k_o[h2], mb * 512, 512), mb == 0, mb == MB - 1),
                                          deps=dep, tag=f"t{t}s{sub}e{e}o{tb}_{h2}"))
                    down_last[h2] = mms_o[-1]
                    stt = S.op("vector",
                               (lambda b, jj, hh, ee: lambda v: v.scalar_tensor_tensor(x_sb[:, b, hh * 512:(hh + 1) * 512], quarters[jj], w_half[:, b, ee:ee + 1], x_sb[:, b, hh * 512:(hh + 1) * 512], ALU.mult, ALU.add))(tb, j, h2, e),
                               deps=[mms_o[-1], w_ops[tb], last_stt_tb[tb]], tag=f"t{t}s{sub}e{e}stt{tb}_{h2}")
                    stt_ops.append(stt)
                    last_stt_tb[tb] = stt
                    q_free[j] = stt
                    if e == NE - 1 and h2 == 1:
                        last_tb_final[tb] = stt
                mark_half_use(k_o[h2], down_last[h2])
            prev_hid = hid_hist
            pfr = [q_free[1], q_free[3]]
        if f"hid@{t}_{sub}" in taps:
            tap(f"hid@{t}_{sub}", hid[:], hid_hist)
        tap(f"x_mid@{t}_{sub}", x_sb[:], last_tb_final)
        return last_tb_final

    def attn_phase(t, hT_deps, attn_sub=2):
        ld_q, k_q = load_half("attn", 0)
        ld_k, k_k = load_half("attn", 1)
        ld_v, k_v = load_half("attn", 2)
        ld_p, k_p = load_half("attn", 3)
        pf = [PF0, PF1]
        phi_ops = []
        for which, dst in ((0, phiq), (1, phik)):
            k_w = k_q if which == 0 else k_k
            ld_w = ld_q if which == 0 else ld_k
            for ib in range(DB):
                par = (which * DB + ib) % 2
                mms = []
                for h2 in range(2):
                    for db in range(DB):
                        dep = []
                        if h2 == 0 and db == 0:
                            dep = [hT_deps[0], ld_w]
                            if len(phi_ops) >= 2:
                                dep.append(phi_ops[-2])
                        elif h2 == 1 and db == 0 and ib == 0 and which == 0:
                            dep = [hT_deps[1]]
                        mms.append(S.op("tensor",
                                        MM(pf[par][:, h2 * 512:(h2 + 1) * 512], hs_col(k_w, db * 1024 + ib * 128, 128), hT[:, db, h2 * 512:(h2 + 1) * 512], db == 0, db == DB - 1),
                                        deps=dep, tag=f"t{t}a_qk{which}_{ib}"))
                if ib == DB - 1:
                    mark_half_use(k_w, mms[-1])
                mn = S.op("vector", (lambda p_: lambda v: v.tensor_scalar_min(tq[:], pf[p_][:], 0.0))(par),
                          deps=[mms[-1]] + phi_ops[-1:], tag=f"t{t}a_min{which}_{ib}")
                exp_ = S.op("scalar", lambda s: s.activation(e_t[:], tq[:], AF.Exp), deps=[mn], tag=f"t{t}a_exp{which}_{ib}")
                phi_ops.append(S.op("vector",
                                    (lambda d_, i_, p_: lambda v: v.scalar_tensor_tensor(d_[:, i_, :], pf[p_][:], 0.0, e_t[:], ALU.max, ALU.add))(dst, ib, par),
                                    deps=[exp_], tag=f"t{t}a_phi{which}_{ib}"))
        v_evs = []
        v_last_mm = None
        for tb in range(NBLK):
            par = tb % 2
            mms = []
            for h2 in range(2):
                for db in range(DB):
                    dep = []
                    if h2 == 0 and db == 0:
                        dep = [ld_v, phi_ops[-2 + par], hT_deps[0 if tb < 4 else 1]]
                        if len(v_evs) >= 2:
                            dep.append(v_evs[-2])
                    mms.append(S.op("tensor",
                                    MM(pf[par][:, h2 * 512:(h2 + 1) * 512], hT[:, db, tb * 128:(tb + 1) * 128], hs_col(k_v, db * 1024 + h2 * 512, 512), db == 0, db == DB - 1),
                                    deps=dep, tag=f"t{t}a_v{tb}"))
            v_last_mm = mms[-1]
            src = bass.AP(pf[par], 0, [[1024, 128], [64, H], [1, 64]])
            v_evs.append(S.op("vector",
                              (lambda b, s_: lambda v: v.tensor_copy(v_tm[:, b, :, 0:64], s_))(tb, src),
                              deps=[mms[-1]], tag=f"t{t}a_vev{tb}"))
        mark_half_use(k_v, v_last_mm)
        ms_vones = S.op("vector", lambda v: v.memset(v_tm[:, :, :, 64:65], 1.0), deps=v_evs, tag=f"t{t}a_vones")
        if attn_sub == 0:
            mark_slot_use(si_vp, v_last_mm)
            return [ms_vones]
        tap(f"phiq@{t}_2", phiq[:], phi_ops)
        tap(f"phik@{t}_2", phik[:], phi_ops)
        tap(f"v_tm@{t}_2", v_tm[:], [ms_vones])
        bsc = S.barrier(f"t{t}_scanbar", extra=[st["last_tap"]])
        prev_xadd = None
        prev_A_ev = [None, None]
        prev_N_read = None
        prev_tk_ev = None
        prev_ty_ev = None
        prev_am_read = [None, None]
        prev_ktm_read = [None]
        prev_ytok_read = [None, None]
        prev_kz_read = [None] * GH
        prev_pproj = None
        xadds = []
        pp_S = S_sb.shape[1] * S_sb.shape[2]
        for c in range(NBLK):
            zred = S.op("vector",
                        (lambda cc: lambda v: v.reduce_sum(zz_col[:], phik[:, :, cc * 128:(cc + 1) * 128], axis=AX.X))(c),
                        deps=[phi_ops[-1], st["zcopy"]], tag=f"t{t}c{c}_zred")
            sc_deps = []
            all_ytr = []
            tk_trs = []
            for pb2 in range(NPAIR):
                dep = []
                if pb2 == 0:
                    dep = [phi_ops[-1], prev_tk_ev]
                tk_trs.append(S.op("tensor",
                                   (lambda p_b, cc: lambda te: te.transpose(PB1[:, p_b, :], phik[:, p_b, cc * 128:(cc + 1) * 128], ident[:]))(pb2, c),
                                   deps=dep, tag=f"t{t}c{c}ktr{pb2}"))
            tk_ev = S.op("vector",
                         lambda v: v.tensor_copy(phik_tm[:], PB1[:]),
                         deps=tk_trs + [prev_ktm_read[0]], tag=f"t{t}c{c}kev")
            prev_tk_ev = tk_ev
            for g in range(NG):
                par = g % 2
                psA = bass.AP(PF1, par * 512, [[1024, 128], [128, GH], [1, 128]])
                kz = []
                for i in range(GH):
                    h = g * GH + i
                    pb, pbase = h // 2, (h % 2) * 64
                    kz.append(S.op("vector",
                                   (lambda p_b, p_base, ii, cc: lambda v: v.tensor_copy(phik_z[p_base:p_base + 64, ii, :], phik[p_base:p_base + 64, p_b, cc * 128:(cc + 1) * 128]))(pb, pbase, i, c),
                                   deps=[phi_ops[-1], prev_kz_read[i]], tag=f"t{t}c{c}g{g}kz{i}"))
                amms = []
                for i in range(GH):
                    h = g * GH + i
                    pb = h // 2
                    dep = [kz[i]] + ([prev_A_ev[par]] if i == 0 else [])
                    amms.append(S.op("tensor",
                                     (lambda p_b, ii, cc, pA: lambda te: te.matmul(pA[:, ii, :], phik_z[:, ii, :], phiq[:, p_b, cc * 128:(cc + 1) * 128], start=True, stop=True))(pb, i, c, psA),
                                     deps=dep, tag=f"t{t}c{c}g{g}A{i}"))
                for i in range(GH):
                    prev_kz_read[i] = amms[i]
                mbc = bass.AP(trimask, 0, [[128, 128], [0, GH], [1, 128]])
                a_ev = S.op("vector",
                            (lambda p_, pA: lambda v: v.tensor_tensor(a_m[:, p_, :, :], pA, mbc, ALU.mult))(par, psA),
                            deps=[amms[-1], prev_am_read[par]], tag=f"t{t}c{c}g{g}Aev")
                prev_A_ev[par] = a_ev
                nmms = []
                for i in range(GH):
                    h = g * GH + i
                    pb = h // 2
                    dep = []
                    if i == 0:
                        dep = [a_ev, ms_vones, prev_N_read, st["scopy"], st["zcopy"]]
                    nmms.append(S.op("tensor",
                                     (lambda hh, ii, cc, p_: lambda te: te.matmul(PF2[:, ii, 0:65], a_m[:, p_, ii, :], v_tm[:, cc, hh, :], start=True, stop=False))(h, i, c, par),
                                     deps=dep, tag=f"t{t}c{c}g{g}n1_{i}"))
                    nmms.append(S.op("tensor",
                                     (lambda hh, p_b, ii, cc: lambda te: te.matmul(PF2[:, ii, 0:65], phiq[:, p_b, cc * 128:(cc + 1) * 128], S_sb[:, hh, :], start=False, stop=True))(h, pb, i, c),
                                     deps=[], tag=f"t{t}c{c}g{g}n2_{i}"))
                prev_am_read[par] = nmms[-2]
                smms = []
                for i in range(GH):
                    h = g * GH + i
                    pb, pbase = h // 2, (h % 2) * 64
                    dep = [tk_ev, st["scopy"]] if i == 0 else []
                    first = st["first_supd_par"][h % 2]
                    st["first_supd_par"][h % 2] = False
                    smms.append(S.op("tensor",
                                     (lambda hh, p_b, p_base, fst, cc: lambda te: te.matmul(psS[p_base:p_base + 64, p_b, :], phik_tm[:, p_b, p_base:p_base + 64], v_tm[:, cc, hh, 0:64], start=fst, stop=False, skip_group_check=True))(h, pb, pbase, first, c),
                                     deps=dep, tag=f"t{t}c{c}g{g}S{i}"))
                prev_ktm_read[0] = smms[-1]
                sc_deps.append(smms[-1])
                den_src = bass.AP(PF2, 64, [[GH * 128, 128], [128, GH]])
                rd1 = S.op("vector", lambda v: v.tensor_scalar_add(rd_s[:], den_src, ATT_EPS),
                           deps=[nmms[-1], prev_N_read], tag=f"t{t}c{c}g{g}rd1")
                S.op("vector", lambda v: v.drain(), tag="dr")
                rd2 = S.op("vector", lambda v: v.reciprocal(rd_s[:], rd_s[:]), deps=[rd1], tag=f"t{t}c{c}g{g}rd2")
                yss = []
                for i in range(GH):
                    h = g * GH + i
                    pl, pbase = (h % 4) // 2, (h % 2) * 64
                    yss.append(S.op("vector",
                                    (lambda ii, p_, p_l, p_base: lambda v: v.tensor_scalar(y_tok[:, p_, p_l, p_base:p_base + 64], PF2[:, ii, 0:64], rd_s[:, ii:ii + 1], None, ALU.mult))(i, par, pl, pbase),
                                    deps=[rd2, prev_ytok_read[par] if i == 0 else None], tag=f"t{t}c{c}g{g}ys{i}"))
                prev_N_read = yss[-1]
                st["pf2_free"] = yss[-1]
                ytr = []
                for j in range(2):
                    dep = [yss[2 * j], yss[2 * j + 1]]
                    if g == 0 and j == 0:
                        dep.append(prev_ty_ev)
                    ytr.append(S.op("tensor",
                                    (lambda gg, jj, p_: lambda te: te.transpose(PB0[:, gg * 2 + jj, :], y_tok[:, p_, jj, :], ident[:]))(g, j, par),
                                    deps=dep, tag=f"t{t}c{c}g{g}ytr{j}"))
                prev_ytok_read[par] = ytr[-1]
                if g == NG - 1:
                    ty_ev = S.op("vector",
                                 lambda v: v.tensor_copy(yT_s[:], PB0[:]),
                                 deps=ytr + all_ytr + [prev_pproj], tag=f"t{t}c{c}ytev")
                    prev_ty_ev = ty_ev
                    last_ytev = ty_ev
                    st["pb_free"] = ty_ev
                all_ytr.extend(ytr)
            sc_e = S.op("vector",
                        lambda v: v.tensor_copy(bass.AP(S_sb, 0, [[pp_S, 64], [2 * 65, NPAIR], [1, 64]]), psS[0:64, :, :]),
                        deps=sc_deps, tag=f"t{t}c{c}_scopyE")
            sc_o = S.op("vector",
                        lambda v: v.tensor_copy(bass.AP(S_sb, 64 * pp_S + 65, [[pp_S, 64], [2 * 65, NPAIR], [1, 64]]), psS[64:128, :, :]),
                        deps=sc_deps, tag=f"t{t}c{c}_scopyO")
            za = S.op("vector", lambda v: v.tensor_add(z_f32[:], z_f32[:], zz_col[:]), deps=[zred], tag=f"t{t}c{c}_zadd")
            S.op("vector", lambda v: v.drain(), tag="dr")
            zc_e = S.op("vector",
                        lambda v: v.tensor_copy(bass.AP(S_sb, 64, [[pp_S, 64], [2 * 65, NPAIR], [1, 1]]),
                                                bass.AP(z_f32, 0, [[NPAIR, 64], [1, NPAIR], [1, 1]])),
                        deps=[za], tag=f"t{t}c{c}_zcE")
            zc_o = S.op("vector",
                        lambda v: v.tensor_copy(bass.AP(S_sb, 64 * pp_S + 65 + 64, [[pp_S, 64], [2 * 65, NPAIR], [1, 1]]),
                                                bass.AP(z_f32, 64 * NPAIR, [[NPAIR, 64], [1, NPAIR], [1, 1]])),
                        deps=[za], tag=f"t{t}c{c}_zcO")
            st["scopy"] = sc_o
            st["zcopy"] = zc_o
            st["scopy_e"] = sc_e
            st["zcopy_e"] = zc_e
            if attn_sub == 1:
                prev_xadd = st["zcopy"]
                continue
            pms = []
            for h2 in range(2):
                for pb in range(NPAIR):
                    dep = []
                    if h2 == 0 and pb == 0:
                        dep = [last_ytev, ld_p, prev_xadd if prev_xadd is not None else phi_ops[-2]]
                    pms.append(S.op("tensor",
                                    MM(PF0[:, h2 * 512:(h2 + 1) * 512], yT_s[:, pb, :], hs_col(k_p, pb * 1024 + h2 * 512, 512), pb == 0, False),
                                    deps=dep, tag=f"t{t}c{c}p{pb}"))
                pms.append(S.op("tensor",
                                MM(PF0[:, h2 * 512:(h2 + 1) * 512], ones_r[0:1, :], bp_row[0:1, h2 * 512:(h2 + 1) * 512], False, True),
                                deps=[], tag=f"t{t}c{c}pb"))
            prev_pproj = pms[-1]
            prev_xadd = S.op("vector",
                             (lambda cc: lambda v: v.tensor_add(x_sb[:, cc, :], x_sb[:, cc, :], PF0[:]))(c),
                             deps=[pms[-1]], tag=f"t{t}c{c}_xadd")
            xadds.append(prev_xadd)
        mark_half_use(k_p, prev_pproj)
        st["xadds"] = xadds
        tap(f"S_sb@{t}_2", S_sb[:], [st["scopy"], st["zcopy"]])
        tap(f"yT_s@{t}_2", yT_s[:], [prev_xadd])
        return xadds if len(xadds) == NBLK else [prev_xadd]

    prev_store = [None] * NBLK
    pending_loads = None
    for t in range(n_tiles):
        if pending_loads is not None:
            x_ld = pending_loads
        else:
            x_ld = []
            for pr in range(NBLK // 2):
                b0 = pr * 2
                x_src = x_in.rearrange("(n p) d -> p n d", p=128)[:, t * NBLK + b0:t * NBLK + b0 + 2, :]
                ldp = S.op("scalar",
                           (lambda xs, b: lambda s: s.dma_start(x_sb[:, b:b + 2, :], xs))(x_src, b0),
                           deps=list(setup_done), dma_sem=f"q_l{pr}", tag=f"t{t}_ldx{pr}")
                x_ld.extend([ldp, ldp])
        ld_x = x_ld[-1]

        cur = x_ld
        x_prev_tb = x_ld
        if level >= 1:
            hT_ev, apps1 = ln_phase(t, 1, cur)
            cur = hT_ev
        if level >= 2:
            w_ops, rev1, rev3 = router_phase(t, 1, hT_ev, apps1, x_prev_tb)
            cur = w_ops
        if level >= 3:
            ffn_last = ffn_phase(t, 1, hT_ev, w_ops, rev1, rev3, apps1, x_prev_tb)
            cur = ffn_last
            x_prev_tb = ffn_last
        if level >= 4:
            hT_ev2, apps2 = ln_phase(t, 2, cur)
            cur = hT_ev2
        if level >= 5:
            attn_last = attn_phase(t, hT_ev2, attn_sub=min(2, level - 5))
            cur = attn_last
            if len(attn_last) == NBLK:
                x_prev_tb = attn_last
        if level >= 8:
            hT_ev3, apps3 = ln_phase(t, 3, cur)
            w_ops3, rev1b, rev3b = router_phase(t, 3, hT_ev3, apps3, x_prev_tb)
            ffn3_last = ffn_phase(t, 3, hT_ev3, w_ops3, rev1b, rev3b, apps3, x_prev_tb)
            cur = ffn3_last
        ffn3_last = cur

        pending_loads = [] if t + 1 < n_tiles else None
        for pr in range(NBLK // 2):
            b0 = pr * 2
            y_dst = y_out.rearrange("(n p) d -> p n d", p=128)[:, t * NBLK + b0:t * NBLK + b0 + 2, :]
            dep = [ffn3_last[b0], ffn3_last[b0 + 1]] if len(ffn3_last) == NBLK else [ffn3_last[-1]]
            stp = S.op("gpsimd",
                       (lambda yd, b: lambda g: g.dma_start(yd, x_sb[:, b:b + 2, :]))(y_dst, b0),
                       deps=dep, dma_sem=f"q_x{pr}", tag=f"t{t}_stx{pr}")
            prev_store[b0] = prev_store[b0 + 1] = stp
            if pending_loads is not None:
                x_src = x_in.rearrange("(n p) d -> p n d", p=128)[:, (t + 1) * NBLK + b0:(t + 1) * NBLK + b0 + 2, :]
                ldp = S.op("scalar",
                           (lambda xs, b: lambda s: s.dma_start(x_sb[:, b:b + 2, :], xs))(x_src, b0),
                           deps=[stp], dma_sem=f"q_l{pr}", tag=f"t{t+1}_ldx{pr}")
                pending_loads.extend([ldp, ldp])
        st["x_store"] = prev_store[-1]

    last_dma = {}
    for eng in ENGINES:
        for o in S.ops[eng]:
            if o.dma_sem:
                last_dma[o.dma_sem] = (eng, o)
    for semname, (eng, o) in sorted(last_dma.items()):
        S.op(eng, lambda g: None, deps=[o, st["x_store"]], tag=f"tail_{semname}")

    S.finalize()
    sems = {name: es.enter_context(nc.semaphore(name)) for name in S.sem_names}
    block = es.enter_context(nc.Block())
    with es:
        block.tensor(lambda t: S.emit("tensor", t, sems))
        block.scalar(lambda s: S.emit("scalar", s, sems))
        block.vector(lambda v: S.emit("vector", v, sems))
        block.gpsimd(lambda g: S.emit("gpsimd", g, sems))
        block.sync(lambda sy: S.emit("sync", sy, sems))
    return nc


_NC_CACHE = {}


def _get_program():
    if "nc" not in _NC_CACHE:
        _NC_CACHE["nc"] = build_program()
    return _NC_CACHE["nc"]


def kernel(**inputs):
    x = np.ascontiguousarray(np.asarray(inputs["L0_attn_x"], dtype=np.float32))
    ident = np.eye(128).astype(ml_dtypes.bfloat16)
    trimask = np.triu(np.ones((128, 128))).astype(ml_dtypes.bfloat16)
    common = {
        "wq": np.ascontiguousarray(np.asarray(inputs["L0_attn_Wq"], dtype=np.float32)),
        "wk": np.ascontiguousarray(np.asarray(inputs["L0_attn_Wk"], dtype=np.float32)),
        "wv": np.ascontiguousarray(np.asarray(inputs["L0_attn_Wv"], dtype=np.float32)),
        "wp": np.ascontiguousarray(np.asarray(inputs["L0_attn_Wp"], dtype=np.float32)),
        "bp": np.ascontiguousarray(np.asarray(inputs["L0_attn_bp"], dtype=np.float32)),
        "wrouter": np.ascontiguousarray(np.asarray(inputs["L0_ffn_W_router"], dtype=np.float32)),
        "wgate": np.ascontiguousarray(np.asarray(inputs["L0_ffn_W_gate"], dtype=np.float32)),
        "wup": np.ascontiguousarray(np.asarray(inputs["L0_ffn_W_up"], dtype=np.float32)),
        "wout": np.ascontiguousarray(np.asarray(inputs["L0_ffn_W_out"], dtype=np.float32)),
        "bgate": np.ascontiguousarray(np.asarray(inputs["L0_ffn_b_gate"], dtype=np.float32)),
        "bup": np.ascontiguousarray(np.asarray(inputs["L0_ffn_b_up"], dtype=np.float32)),
        "bout": np.ascontiguousarray(np.asarray(inputs["L0_ffn_b_out"], dtype=np.float32)),
        "ident": ident,
        "trimask": trimask,
    }
    nc = _get_program()
    in_maps = [dict(common, x=np.ascontiguousarray(x[c])) for c in range(8)]
    res = run_bass_kernel_spmd(nc, in_maps, core_ids=list(range(8)))
    return np.stack([np.asarray(res.results[c]["y"], dtype=np.float32) for c in range(8)], axis=0)


if __name__ == "__main__":
    nc = build_program()
    n_ops = sum(1 for _ in ())
    print("built ok")


# version 13
# speedup vs baseline: 1.1477x; 1.1477x over previous
import numpy as np
import ml_dtypes
from contextlib import ExitStack

import concourse.bass as bass
import concourse.mybir as mybir
from concourse.bass_utils import run_bass_kernel_spmd
import concourse.bass_utils as _bu
from pathlib import Path as _Path


def _bvo_ldwopt(tmpdir, inp="bir.json", outp="file.neff", arch=None, *, dve_root=None):
    cmd = [
        _bu.get_walrus_driver(),
        "--pass",
        ",".join([
            "birverifier", "runtime_memory_reservation", "lower_act",
            "lower_dve", "lower_ap_offset", "codegen", "neff_packager",
        ]),
        "-i", inp,
        "--neff-output-filename", outp,
        "--enable-birsim=true",
        "--mem-mode=physical",
        "--policy=0",
        "--enable-ldw-opt=true",
        "--assign-static-dmas-to-sp=false",
        f"--dram-page-size={_bu.aot_getenv('NEURON_SCRATCHPAD_PAGE_SIZE', '256')}",
        f"--enable-neff-debug-info={'false' if _bu.aot_checkenv('CONCOURSE_SCRUB_NEFF_DEBUG_INFO') else 'true'}",
        "--jobs", "8",
        *_bu.get_walrus_args(
            _bu.get_bir_arch(tmpdir, inp) if arch is None else arch,
            tmpdir, dve_root=dve_root,
        ),
    ]
    result = _bu.run_command(cmd, cwd=tmpdir)
    if result is not None:
        (_Path(tmpdir) / "log.txt").write_text(result.stdout)
    return f"{tmpdir}/{outp}"


_bu.bir_verify_and_optimise = _bvo_ldwopt

F32 = mybir.dt.float32
BF16 = mybir.dt.bfloat16
AF = mybir.ActivationFunctionType
ALU = mybir.AluOpType
AX = mybir.AxisListType

SEQ, D, TILE = 8192, 1024, 1024
NT = SEQ // TILE
NBLK = TILE // 128
DB = D // 128
M, NE, H, HD = 2048, 8, 16, 64
MB = M // 128
NPAIR = 8
NG, GH = 4, 4
LN_EPS, ATT_EPS, ROUTER_EPS = 1e-5, 1e-6, 1e-8

ENGINES = ("tensor", "scalar", "vector", "gpsimd", "sync")
COMPUTE_ENGINES = ("tensor", "scalar", "vector")


class Op:
    __slots__ = ("eng", "fn", "deps", "dma_sem", "flagged", "value", "tag")

    def __init__(self, eng, fn, deps, dma_sem, tag):
        self.eng, self.fn, self.deps, self.dma_sem, self.tag = eng, fn, deps, dma_sem, tag
        self.flagged = dma_sem is not None
        self.value = None


class Sched:
    def __init__(self):
        self.ops = {e: [] for e in ENGINES}
        self.last = {e: None for e in ENGINES}

    def op(self, eng, fn, deps=(), dma_sem=None, tag=""):
        deps = [d for d in deps if d is not None]
        o = Op(eng, fn, deps, dma_sem, tag)
        for d in deps:
            d.flagged = True
        self.ops[eng].append(o)
        self.last[eng] = o
        return o

    def barrier(self, tag="bar", extra=()):
        snap = {e: self.last[e] for e in COMPUTE_ENGINES}
        outs = {}
        for e in COMPUTE_ENGINES:
            deps = [snap[o] for o in COMPUTE_ENGINES if snap[o] is not None]
            deps += [d for d in extra if d is not None]
            outs[e] = self.op(e, lambda _: None, deps=deps, tag=tag + "_" + e)
        return outs

    def finalize(self):
        counters = {}
        for eng in ENGINES:
            for o in self.ops[eng]:
                if not o.flagged:
                    continue
                sem = o.dma_sem if o.dma_sem else "s_" + eng
                step = 16 if o.dma_sem else 1
                counters[sem] = counters.get(sem, 0) + step
                o.value = (sem, counters[sem])
        self.sem_names = sorted(counters)
        self.final_counts = counters

    def emit(self, eng, eobj, sems):
        waited = {}
        for o in self.ops[eng]:
            need = {}
            for d in o.deps:
                sem, val = d.value
                if val > need.get(sem, 0):
                    need[sem] = val
            for sem, val in need.items():
                if waited.get(sem, 0) >= val:
                    continue
                waited[sem] = val
                eobj.wait_ge(sems[sem], val)
            inst = o.fn(eobj)
            if o.flagged:
                sem, _ = o.value
                if inst is None:
                    inst = eobj.nop()
                inst.then_inc(sems[sem], 16 if o.dma_sem else 1)


def build_program(debug_taps=(), n_tiles=NT, level=8):
    nc = bass.Bass("TRN2", target_bir_lowering=False)

    seq = n_tiles * TILE
    x_in = nc.dram_tensor("x", [seq, D], F32, kind="ExternalInput").ap()
    w_in = {}
    for nm, shp in (("wq", [D, D]), ("wk", [D, D]), ("wv", [D, D]), ("wp", [D, D]),
                    ("bp", [D]), ("wrouter", [D, NE]),
                    ("wgate", [NE, D, M]), ("wup", [NE, D, M]), ("wout", [NE, M, D]),
                    ("bgate", [NE, M]), ("bup", [NE, M]), ("bout", [NE, D])):
        w_in[nm] = nc.dram_tensor(nm, shp, F32, kind="ExternalInput").ap()
    ident_in = nc.dram_tensor("ident", [128, 128], BF16, kind="ExternalInput").ap()
    trimask_in = nc.dram_tensor("trimask", [128, 128], BF16, kind="ExternalInput").ap()
    y_out = nc.dram_tensor("y", [seq, D], F32, kind="ExternalOutput").ap()

    scr_gate = nc.dram_tensor("scr_gate", [NE, D, M], BF16, kind="Internal").ap()
    scr_up = nc.dram_tensor("scr_up", [NE, D, M], BF16, kind="Internal").ap()
    scr_out = nc.dram_tensor("scr_out", [NE, M, D], BF16, kind="Internal").ap()
    scr_attn = nc.dram_tensor("scr_attn", [4, D, D], BF16, kind="Internal").ap()

    taps = {}
    tap_shapes = {
        "h_tok": ([128, NBLK, D], BF16), "hT": ([128, DB, TILE], BF16),
        "w_half": ([128, NBLK, NE], F32), "x_mid": ([128, NBLK, D], F32),
        "phiq": ([128, DB, TILE], BF16), "phik": ([128, DB, TILE], BF16),
        "v_tm": ([128, NBLK, H, 65], BF16), "S_sb": ([128, H, 65], BF16),
        "yT_s": ([128, NPAIR, 128], BF16), "hid": ([128, MB, TILE], BF16),
    }
    for name in debug_taps:
        shp, dt = tap_shapes[name.split("@")[0]]
        taps[name] = nc.dram_tensor("tap_" + name.replace("@", "_"), shp, dt,
                                    kind="ExternalOutput").ap()

    es = ExitStack()

    ARENA_PP = 204 * 1024
    arena = nc.alloc_sbuf_tensor("arena", [128, ARENA_PP], mybir.dt.uint8)
    arena_base = nc.lookup_mloc(arena).addr
    offset_cursor = [0]

    def at(name, shape, dtype, rel=None, align=64):
        if rel is None:
            offset_cursor[0] = (offset_cursor[0] + align - 1) // align * align
            rel = offset_cursor[0]
            sz = int(np.prod(shape[1:])) * mybir.dt.size(dtype)
            offset_cursor[0] += sz
        assert rel + int(np.prod(shape[1:])) * mybir.dt.size(dtype) <= ARENA_PP, name
        h = nc.alloc_sbuf_tensor_at(name, list(shape), dtype, offset=arena_base + rel)
        return h, rel

    def at_(name, shape, dtype, rel=None, align=64):
        return at(name, shape, dtype, rel, align)[0]

    x_sb = at_("x_sb", [128, NBLK, D], F32)
    sq_col = at_("sq_col", [128, NBLK], F32)
    sum_col = at_("sum_col", [128, NBLK], F32)
    mcol = at_("mcol", [128, NBLK], F32)
    v1col = at_("v1col", [128, NBLK], F32)
    m2col = at_("m2col", [128, NBLK], F32)
    varcol = at_("varcol", [128, NBLK], F32)
    rstd = at_("rstd", [128, NBLK], F32)
    nmr = at_("nmr", [128, NBLK], F32)
    zz_col = at_("zz_col", [128, NPAIR], F32)
    z_f32 = at_("z_f32", [128, NPAIR], F32)
    lg_tok = at_("lg_tok", [128, NBLK, NE], F32)
    el_t = at_("el_t", [128, NBLK, NE], F32)
    wr_t = at_("wr_t", [128, NBLK, NE], F32)
    w_half = at_("w_half", [128, NBLK, NE], F32)
    pm_t = at_("pm_t", [128, NBLK, NE], F32)
    keep_t = at_("keep_t", [128, NBLK, NE], F32)
    mx_r = at_("mx_r", [128, NBLK], F32)
    s8_r = at_("s8_r", [128, NBLK], F32)
    m1_r = at_("m1_r", [128, NBLK], F32)
    m2_r = at_("m2_r", [128, NBLK], F32)
    s2_r = at_("s2_r", [128, NBLK], F32)
    den_r = at_("den_r", [128, NBLK], F32)
    rden_r = at_("rden_r", [128, NBLK], F32)
    negmx = at_("negmx", [128, NBLK], F32)
    rd_s = at_("rd_s", [128, GH], F32)
    eps_c = at_("eps_c", [128, 1], F32)
    neghalf_c = at_("neghalf_c", [128, 1], F32)
    bgate_col = at_("bgate_col", [128, MB, NE], F32)
    bup_col = at_("bup_col", [128, MB, NE], F32)

    h_tok, o_htok = at("h_tok", [128, NBLK, D], BF16)
    v_tm = at_("v_tm", [128, NBLK, H, 65], BF16, rel=o_htok)
    offset_cursor[0] = max(offset_cursor[0], o_htok + NBLK * H * 65 * 2)
    hT = at_("hT", [128, DB, TILE], BF16)
    hid, o_hid = at("hid", [128, MB, TILE], BF16)
    phiq = at_("phiq", [128, DB, TILE], BF16, rel=o_hid)
    phik = at_("phik", [128, DB, TILE], BF16, rel=o_hid + DB * TILE * 2)
    wslotA, o_wA = at("wslotA", [128, 16384], BF16)
    wslotB, o_wB = at("wslotB", [128, 16384], BF16)
    bg_stage = at_("bg_stage", [NE, M], BF16, rel=o_wA)
    bu_stage = at_("bu_stage", [NE, M], BF16, rel=o_wA + M * 2)
    gtmp, o_gtmp = at("gtmp", [128, 4, TILE], BF16)
    xsq = at_("xsq", [128, TILE], F32, rel=o_gtmp)
    e_t = at_("e_t", [128, TILE], F32, rel=o_gtmp)
    o_utmp = o_gtmp + 2 * TILE * 2
    tq = at_("tq", [128, TILE], F32, rel=o_utmp)
    lgT_s = at_("lgT_s", [NE, TILE], BF16, rel=o_utmp)
    a_m = at_("a_m", [128, 2, GH, 128], BF16)
    phik_tm = at_("phik_tm", [128, NPAIR, 128], BF16)
    phik_z = at_("phik_z", [128, GH, 128], BF16)
    y_tok = at_("y_tok", [128, 2, 2, 128], BF16)
    yT_s = at_("yT_s", [128, NPAIR, 128], BF16)
    S_sb = at_("S_sb", [128, H, 65], BF16)
    bout_mat = at_("bout_mat", [NE, D], BF16)
    w_bf = at_("w_bf", [128, NBLK, NE], BF16)
    wT_s = at_("wT_s", [NE, NBLK, 128], BF16)
    bp_row = at_("bp_row", [1, D], BF16)
    wrouter_sb = at_("wrouter_sb", [128, DB, NE], BF16)
    trimask = at_("trimask", [128, 128], BF16)
    ident = at_("ident", [128, 128], BF16)
    ones_r = at_("ones_r", [1, 128], BF16)

    assert offset_cursor[0] <= ARENA_PP, offset_cursor[0]

    psS = es.enter_context(nc.psum_tensor("psS", [128, NPAIR, HD], F32))
    PF0 = es.enter_context(nc.psum_tensor("PF0", [128, 1024], F32))
    PF1 = es.enter_context(nc.psum_tensor("PF1", [128, 1024], F32))
    PB0 = es.enter_context(nc.psum_tensor("PB0", [128, NBLK, 128], BF16))
    PB1 = es.enter_context(nc.psum_tensor("PB1", [128, NBLK, 128], BF16))
    PF2 = es.enter_context(nc.psum_tensor("PF2", [128, GH, 128], F32))

    S = Sched()

    def slot3(slot, nb, width, ofs=0):
        return bass.AP(slot, ofs, [[16384, 128], [width, nb], [1, width]])

    def MM(out_ap, lhsT_ap, rhs_ap, start, stop):
        return lambda t: t.matmul(out_ap, lhsT_ap, rhs_ap, start=start, stop=stop)

    def TR(out_ap, in_ap, id_ap):
        return lambda t: t.transpose(out_ap, in_ap, id_ap)

    setup_casts = []
    for e in range(NE):
        for src, dst, nm in ((w_in["wgate"][e], scr_gate[e], "g"),
                             (w_in["wup"][e], scr_up[e], "u"),
                             (w_in["wout"][e], scr_out[e], "o")):
            setup_casts.append(S.op("gpsimd", (lambda s_, d_: lambda g: g.dma_start(d_, s_))(src, dst),
                                    dma_sem="q_set", tag=f"cast_{nm}{e}"))

    ld_tri = S.op("gpsimd", lambda g: g.dma_start(trimask[:], trimask_in), dma_sem="q_set", tag="ld_tri")
    ld_id = S.op("gpsimd", lambda g: g.dma_start(ident[:], ident_in), dma_sem="q_set", tag="ld_id")
    ld_bout = S.op("gpsimd", lambda g: g.dma_start(bout_mat[:], w_in["bout"]), dma_sem="q_set", tag="ld_bout")
    ld_bp = S.op("gpsimd", lambda g: g.dma_start(bp_row[0:1, :], w_in["bp"][None, :]), dma_sem="q_set", tag="ld_bp")
    ld_wrt = S.op("gpsimd", lambda g: g.dma_start(wrouter_sb[:], w_in["wrouter"].rearrange("(b p) e -> p b e", p=128)), dma_sem="q_set", tag="ld_wrt")
    ld_bg = S.op("gpsimd", lambda g: g.dma_start(bg_stage[:], w_in["bgate"]), dma_sem="q_set", tag="ld_bg")
    ld_bu = S.op("gpsimd", lambda g: g.dma_start(bu_stage[:], w_in["bup"]), dma_sem="q_set", tag="ld_bu")
    ms_ones = S.op("gpsimd", lambda g: g.memset(ones_r[:], 1.0), tag="ms_ones")
    setup_fence = S.op("gpsimd", lambda g: None,
                       deps=[setup_casts[-1], ld_tri, ld_id, ld_bout, ld_bp, ld_wrt,
                             ld_bg, ld_bu, ms_ones], tag="setup_fence")
    ms_S = S.op("vector", lambda v: v.memset(S_sb[:], 0.0), tag="ms_S")
    ms_kz = S.op("vector", lambda v: v.memset(phik_z[:], 0.0), tag="ms_kz")
    ms_eps = S.op("vector", lambda v: v.memset(eps_c[:], LN_EPS), tag="ms_eps")
    ms_nh = S.op("vector", lambda v: v.memset(neghalf_c[:], -0.5), tag="ms_nh")
    ms_Z = S.op("vector", lambda v: v.memset(z_f32[:], 0.0), tag="ms_Z")

    prev = None
    for stage, col, nm in ((bg_stage, bgate_col, "g"), (bu_stage, bup_col, "u")):
        for mb in range(MB):
            tr = S.op("tensor",
                      (lambda st, m: lambda t: t.transpose(PB1[:, 0, 0:8], st[0:NE, m * 128:(m + 1) * 128], ident[0:NE, 0:NE]))(stage, mb),
                      deps=[setup_fence, prev], tag=f"btr_{nm}{mb}")
            prev = S.op("vector",
                        (lambda m, c: lambda v: v.tensor_copy(c[:, m, :], PB1[:, 0, 0:8]))(mb, col),
                        deps=[tr], tag=f"bev_{nm}{mb}")
    bias_done = prev

    srcs = (w_in["wq"], w_in["wk"], w_in["wv"], w_in["wp"])
    aw_store = []
    prev_store = bias_done
    for mi in range(4):
        ld = S.op("gpsimd",
                  (lambda ws: lambda g: g.dma_start(slot3(wslotA, DB, D), ws.rearrange("(b p) d -> p b d", p=128)))(srcs[mi]),
                  deps=[prev_store], dma_sem="q_aw", tag=f"aw_ld{mi}")
        prev_ev = None
        evs = []
        for fill in range(8):
            trs = []
            for k in range(8):
                idx = fill * 8 + k
                ib, db = idx // DB, idx % DB
                trs.append(S.op("tensor",
                                (lambda i_, d_, kk: lambda t: t.transpose(PB0[:, kk, :], wslotA[:, i_ * D + d_ * 128:i_ * D + (d_ + 1) * 128], ident[:]))(ib, db, k),
                                deps=[ld, setup_fence, prev_ev] if k == 0 else [], tag=f"awtr{mi}_{idx}"))
            def mk_ev(fill):
                def f(v):
                    last = None
                    for k in range(8):
                        idx = fill * 8 + k
                        ib, db = idx // DB, idx % DB
                        last = v.tensor_copy(wslotB[:, db * D + ib * 128: db * D + (ib + 1) * 128], PB0[:, k, :])
                    return last
                return f
            prev_ev = S.op("vector", mk_ev(fill), deps=trs, tag=f"awev{mi}_{fill}")
            evs.append(prev_ev)
        st = S.op("sync",
                  (lambda m_: lambda sy: sy.dma_start(scr_attn[m_].rearrange("(b p) d -> p b d", p=128), slot3(wslotB, DB, D)))(mi),
                  deps=evs, dma_sem="q_scr", tag=f"awst{mi}")
        prev_store = st
        aw_store.append(st)

    setup_done = [aw_store[-1], setup_fence, ms_S, ms_kz, ms_Z, ms_eps, ms_nh, bias_done]
    st_pb_init = prev_ev

    HS = 8192
    hs_tensor = [wslotA, wslotA, wslotB, wslotB]
    hs_off = [0, HS, 0, HS]
    hs_sems = ["q_w0", "q_w1", "q_w2", "q_w3"]
    hs_last_use = [None, None, None, None]
    hs_seq = [0]

    def hs_view(k, pattern):
        return bass.AP(hs_tensor[k], hs_off[k], [[16384, 128]] + pattern)

    def hs_col(k, c0, n):
        return bass.AP(hs_tensor[k], hs_off[k] + c0, [[16384, 128], [1, n]])

    def load_half(kind, e=0, h=0):
        k = hs_seq[0] % 4
        hs_seq[0] += 1
        deps = [hs_last_use[k]] if hs_last_use[k] is not None else list(setup_done)
        if kind in ("gate", "up"):
            scr = scr_gate if kind == "gate" else scr_up
            src = scr[e].rearrange("(b p) m -> p b m", p=128)[:, :, h * 1024:(h + 1) * 1024]
            dst = hs_view(k, [[1024, DB], [1, 1024]])
        elif kind == "out":
            src = scr_out[e].rearrange("(b p) d -> p b d", p=128)[:, :, h * 512:(h + 1) * 512]
            dst = hs_view(k, [[512, MB], [1, 512]])
        elif kind == "attn":
            src = scr_attn[e].rearrange("(b p) d -> p b d", p=128)
            dst = hs_view(k, [[1024, DB], [1, 1024]])
        fn = (lambda d_, s_: lambda g: g.dma_start(d_, s_))(dst, src)
        eng = "gpsimd" if kind == "up" else "sync"
        sem = hs_sems[k] + ("u" if eng == "gpsimd" else "")
        op = S.op(eng, fn, deps=deps, dma_sem=sem, tag=f"wld_{kind}{e}_{h}")
        return op, k

    def mark_half_use(k, op):
        hs_last_use[k] = op

    st = {
        "scopy": None, "zcopy": None, "first_supd": True,
        "x_store": None, "last_tap": None, "first_supd_par": [True, True],
        "pb_free": None, "pf2_free": None,
    }
    st["pb_free"] = st_pb_init

    def tap(name, src_ap, deps):
        if name in taps:
            op = S.op("sync", (lambda d_, s_: lambda sy: sy.dma_start(d_, s_))(taps[name], src_ap),
                      deps=deps, dma_sem="q_tap", tag="tap_" + name)
            st["last_tap"] = op
            return op
        return None

    def ln_phase(t, sub, x_deps):
        tap_dep = st["last_tap"]
        per_tb = list(x_deps) if len(x_deps) == NBLK else [x_deps[-1]] * NBLK
        sq_ops, red_ops = [], []
        for tb in range(NBLK):
            sq_ops.append(S.op("scalar",
                               (lambda b: lambda s: s.activation(xsq[:, :], x_sb[:, b, :], AF.Square, accum_out=sq_col[:, b:b + 1]))(tb),
                               deps=[per_tb[tb], tap_dep] + sq_ops[-1:], tag=f"t{t}s{sub}_sq{tb}"))
            red_ops.append(S.op("vector",
                                (lambda b: lambda v: v.reduce_sum(sum_col[:, b:b + 1], x_sb[:, b, :], axis=AX.X))(tb),
                                deps=[per_tb[tb], tap_dep], tag=f"t{t}s{sub}_red{tb}"))
        o_m = [S.op("vector", (lambda b: lambda v: v.tensor_scalar_mul(mcol[:, b:b + 1], sum_col[:, b:b + 1], 1.0 / D))(tb),
                    deps=[red_ops[tb]], tag=f"t{t}s{sub}_m{tb}") for tb in range(NBLK)]
        o_v1 = [S.op("vector", (lambda b: lambda v: v.tensor_scalar_mul(v1col[:, b:b + 1], sq_col[:, b:b + 1], 1.0 / D))(tb),
                     deps=[sq_ops[tb]], tag=f"t{t}s{sub}_v1{tb}") for tb in range(NBLK)]
        o_m2 = [S.op("vector", (lambda b: lambda v: v.tensor_mul(m2col[:, b:b + 1], mcol[:, b:b + 1], mcol[:, b:b + 1]))(tb),
                     deps=[o_m[tb]], tag=f"t{t}s{sub}_m2{tb}") for tb in range(NBLK)]
        o_var = [S.op("vector", (lambda b: lambda v: v.tensor_sub(varcol[:, b:b + 1], v1col[:, b:b + 1], m2col[:, b:b + 1]))(tb),
                      deps=[o_v1[tb], o_m2[tb]], tag=f"t{t}s{sub}_var{tb}") for tb in range(NBLK)]
        o_ln = [S.op("scalar", (lambda b: lambda s: s.activation(varcol[:, b:b + 1], varcol[:, b:b + 1], AF.Ln, bias=eps_c[:, 0:1]))(tb),
                     deps=[o_var[tb]], tag=f"t{t}s{sub}_ln{tb}") for tb in range(NBLK)]
        o_ex = [S.op("scalar", (lambda b: lambda s: s.activation(rstd[:, b:b + 1], varcol[:, b:b + 1], AF.Exp, scale=neghalf_c[:, 0:1]))(tb),
                     deps=[o_ln[tb]], tag=f"t{t}s{sub}_ex{tb}") for tb in range(NBLK)]
        o_nm = [S.op("vector", (lambda b: lambda v: v.scalar_tensor_tensor(nmr[:, b:b + 1], mcol[:, b:b + 1], -1.0, rstd[:, b:b + 1], ALU.mult, ALU.mult))(tb),
                     deps=[o_ex[tb]], tag=f"t{t}s{sub}_nm{tb}") for tb in range(NBLK)]
        app_ops = []
        for tb in range(NBLK):
            app_ops.append(S.op("scalar",
                                (lambda b: lambda s: s.activation(h_tok[:, b, :], x_sb[:, b, :], AF.Identity, bias=nmr[:, b:b + 1], scale=rstd[:, b:b + 1]))(tb),
                                deps=[o_nm[tb], o_ex[tb]], tag=f"t{t}s{sub}_app{tb}"))
        ev_ops = []
        idx = 0
        for fill in range(16):
            ps = PB0 if fill % 2 == 0 else PB1
            trs = []
            for k in range(4):
                tb, db = idx // DB, idx % DB
                dep = [app_ops[tb]]
                if fill < 2 and k == 0:
                    dep.append(st["pb_free"])
                if len(ev_ops) >= 2:
                    dep.append(ev_ops[-2])
                trs.append(S.op("tensor",
                                (lambda b, d_, kk, p_: lambda te: te.transpose(p_[:, kk, :], h_tok[:, b, d_ * 128:(d_ + 1) * 128], ident[:]))(tb, db, k, ps),
                                deps=dep, tag=f"t{t}s{sub}_htr{idx}"))
                idx += 1
            def mk_ev(fill, ps):
                def f(v):
                    last = None
                    for k in range(4):
                        i2 = fill * 4 + k
                        tb, db = i2 // DB, i2 % DB
                        last = v.tensor_copy(hT[:, db, tb * 128:(tb + 1) * 128], ps[:, k, :])
                    return last
                return f
            ev_ops.append(S.op("vector", mk_ev(fill, ps), deps=trs, tag=f"t{t}s{sub}_hev{fill}"))
        st["pb_free"] = ev_ops[-1]
        tap(f"h_tok@{t}_{sub}", h_tok[:], app_ops)
        tap(f"hT@{t}_{sub}", hT[:], ev_ops)
        return [ev_ops[7], ev_ops[15]], app_ops

    def router_phase(t, sub, hT_deps, app_ops, x_prev):
        b0 = S.barrier(f"t{t}s{sub}_rtbar", extra=[st["last_tap"]])
        lg_mms = []
        for h2 in range(2):
            for db in range(DB):
                dep = []
                if db == 0:
                    dep = [hT_deps[h2]] + ([setup_fence] if h2 == 0 else [])
                lg_mms.append(S.op("tensor",
                                   MM(PF0[0:NE, h2 * 512:(h2 + 1) * 512], wrouter_sb[:, db, :], hT[:, db, h2 * 512:(h2 + 1) * 512], db == 0, db == DB - 1),
                                   deps=dep, tag=f"t{t}s{sub}_lgmm"))
        ev1 = S.op("vector", lambda v: v.tensor_copy(lgT_s[:, :], PF0[0:NE, :]), deps=[lg_mms[-1]], tag=f"t{t}s{sub}_lgev")
        trs = []
        for tb in range(NBLK):
            trs.append(S.op("tensor",
                            (lambda b: lambda te: te.transpose(PB0[:, b, 0:8], lgT_s[0:NE, b * 128:(b + 1) * 128], ident[0:NE, 0:NE]))(tb),
                            deps=[ev1] if tb == 0 else [], tag=f"t{t}s{sub}_lgtr{tb}"))
        lg_src = bass.AP(PB0, 0, [[NBLK * 128, 128], [128, NBLK], [1, NE]])
        ev2 = S.op("vector", lambda v: v.tensor_copy(lg_tok[:], lg_src), deps=trs, tag=f"t{t}s{sub}_lgev2")

        def batch(opname, mk, deps_of):
            return [S.op("vector", mk(b), deps=deps_of(b), tag=f"t{t}s{sub}_{opname}{b}") for b in range(NBLK)]

        o_mx = batch("mx", lambda b: lambda v: v.reduce_max(mx_r[:, b:b + 1], lg_tok[:, b, :], axis=AX.X), lambda b: [ev2])
        o_ng = batch("ng", lambda b: lambda v: v.tensor_scalar_mul(negmx[:, b:b + 1], mx_r[:, b:b + 1], -1.0), lambda b: [o_mx[b]])
        o_el = [S.op("scalar",
                     (lambda b: lambda s: s.activation(el_t[:, b, :], lg_tok[:, b, :], AF.Exp, bias=negmx[:, b:b + 1]))(tb2),
                     deps=[o_ng[tb2], ev2], tag=f"t{t}s{sub}_el{tb2}") for tb2 in range(NBLK)]
        o_s8 = batch("s8", lambda b: lambda v: v.reduce_sum(s8_r[:, b:b + 1], el_t[:, b, :], axis=AX.X), lambda b: [o_el[b]])
        o_m1 = batch("m1", lambda b: lambda v: v.reduce_max(m1_r[:, b:b + 1], el_t[:, b, :], axis=AX.X), lambda b: [o_el[b]])
        o_pmm = batch("pmm", lambda b: lambda v: v.tensor_scalar(pm_t[:, b, :], el_t[:, b, :], m1_r[:, b:b + 1], None, ALU.is_lt), lambda b: [o_m1[b]])
        o_pm = batch("pm", lambda b: lambda v: v.tensor_mul(pm_t[:, b, :], pm_t[:, b, :], el_t[:, b, :]), lambda b: [o_pmm[b]])
        o_m2 = batch("m2", lambda b: lambda v: v.reduce_max(m2_r[:, b:b + 1], pm_t[:, b, :], axis=AX.X), lambda b: [o_pm[b]])
        o_kp = batch("kp", lambda b: lambda v: v.tensor_scalar(keep_t[:, b, :], el_t[:, b, :], m2_r[:, b:b + 1], None, ALU.is_ge), lambda b: [o_m2[b]])
        o_wr = batch("wr", lambda b: lambda v: v.tensor_mul(wr_t[:, b, :], keep_t[:, b, :], el_t[:, b, :]), lambda b: [o_kp[b]])
        o_s2 = batch("s2", lambda b: lambda v: v.reduce_sum(s2_r[:, b:b + 1], wr_t[:, b, :], axis=AX.X), lambda b: [o_wr[b]])
        o_dn = batch("dn", lambda b: lambda v: v.scalar_tensor_tensor(den_r[:, b:b + 1], s8_r[:, b:b + 1], ROUTER_EPS, s2_r[:, b:b + 1], ALU.mult, ALU.add), lambda b: [o_s2[b], o_s8[b]])
        o_rd = batch("rd", lambda b: lambda v: v.reciprocal(rden_r[:, b:b + 1], den_r[:, b:b + 1]), lambda b: [o_dn[b]])
        o_wh = batch("wh", lambda b: lambda v: v.tensor_scalar(w_half[:, b, :], wr_t[:, b, :], rden_r[:, b:b + 1], 0.5, ALU.mult, ALU.mult), lambda b: [o_rd[b]])
        tap(f"w_half@{t}_{sub}", w_half[:], o_wh)
        wbf = S.op("vector", lambda v: v.tensor_copy(w_bf[:], w_half[:]), deps=o_wh, tag=f"t{t}s{sub}_wbf")
        trs2 = []
        for tb in range(NBLK):
            dep = [wbf] if tb > 0 else [wbf, ev2]
            trs2.append(S.op("tensor",
                             (lambda b: lambda te: te.transpose(PB0[0:NE, b, :], w_bf[:, b, :], ident[:]))(tb),
                             deps=dep, tag=f"t{t}s{sub}_wtr{tb}"))
        ev3 = S.op("vector", lambda v: v.tensor_copy(wT_s[:], PB0[0:NE, :, :]), deps=trs2, tag=f"t{t}s{sub}_wTev")
        st["pb_free"] = ev3
        return o_wh, ev1, ev3

    def ffn_phase(t, sub, hT_deps, w_ops, ev1, ev3, app_ops, x_prev):
        pfr = [ev1, None]
        stt_ops = []
        last_stt_tb = list(x_prev)
        hid_hist = [None] * MB
        prev_hid = [None] * MB
        last_tb_final = [None] * NBLK
        PF2_flat = bass.AP(PF2, 0, [[512, 128], [1, 512]])
        bias_state = {"prev": st.get("pf2_free"), "i": 0}

        def emit_bias_piece():
            i = bias_state["i"]
            if i >= 16:
                return
            bias_state["i"] += 1
            tb, h2 = i // 2, i % 2
            bmm = S.op("tensor",
                       (lambda b, h_: lambda te: te.matmul(PF2_flat, wT_s[0:NE, b, :], bout_mat[0:NE, h_ * 512:(h_ + 1) * 512], start=True, stop=True))(tb, h2),
                       deps=[ev3, setup_fence, bias_state["prev"]], tag=f"t{t}s{sub}_bmm{tb}_{h2}")
            badd = S.op("vector",
                        (lambda b, h_: lambda v: v.tensor_add(x_sb[:, b, h_ * 512:(h_ + 1) * 512], x_sb[:, b, h_ * 512:(h_ + 1) * 512], PF2_flat))(tb, h2),
                        deps=[bmm, app_ops[tb], x_prev[tb], last_stt_tb[tb]], tag=f"t{t}s{sub}_badd{tb}_{h2}")
            bias_state["prev"] = badd
            last_stt_tb[tb] = badd
            st["pf2_free"] = badd

        for e in range(NE):
            ld_gA, k_gA = load_half("gate", e, 0)
            ld_gB, k_gB = load_half("gate", e, 1)
            ld_uA, k_uA = load_half("up", e, 0)
            ld_uB, k_uB = load_half("up", e, 1)
            ld_o = [None, None]
            k_o = [None, None]
            ev_g = [None] * MB
            hid_hist = [None] * MB
            if e == 0:
                ev_gh = [[None] * MB, [None] * MB]
                hid_h = [[None] * MB, [None] * MB]
                for h2 in range(2):
                    for mb in range(MB):
                        k_g = k_gA if mb < 8 else k_gB
                        k_u = k_uA if mb < 8 else k_uB
                        ld_g = ld_gA if mb < 8 else ld_gB
                        ld_u = ld_uA if mb < 8 else ld_uB
                        mcol0 = (mb % 8) * 128
                        par = mb % 2
                        gq = PF0[:, (par + 0) * 512 - par * 512:][:, 0:512] if False else PF0[:, par * 512:(par + 1) * 512]
                        uq = PF1[:, par * 512:(par + 1) * 512]
                        mms = []
                        for db in range(DB):
                            dep = []
                            if db == 0:
                                dep = [hT_deps[h2], ld_g]
                                dep.append(ev_gh[h2][mb - 2] if mb >= 2 else (pfr[0] if h2 == 0 else ev_gh[0][MB - 2 + mb]))
                            mms.append(S.op("tensor",
                                            MM(gq, hs_col(k_g, db * 1024 + mcol0, 128), hT[:, db, h2 * 512:(h2 + 1) * 512], db == 0, db == DB - 1),
                                            deps=dep, tag=f"t{t}s{sub}e{e}g{mb}"))
                        if h2 == 1:
                            mark_half_use(k_g, mms[-1])
                        if h2 == 0 and mb >= 8:
                            emit_bias_piece()
                            emit_bias_piece()
                        if h2 == 1 and mb == 7:
                            ld_o[0], k_o[0] = load_half("out", e, 0)
                        dep_ge = [mms[-1], ev_gh[h2][mb - 2] if mb >= 2 else None,
                                  hid_h[h2][mb - 4] if mb >= 4 else (hid_h[0][MB - 4 + mb] if h2 == 1 else prev_hid[MB - 4 + mb])]
                        ev_gh[h2][mb] = S.op("scalar",
                                             (lambda m, ee, h_: lambda s: s.activation(gtmp[:, m % 4, h_ * 512:(h_ + 1) * 512], PF0[:, (m % 2) * 512:(m % 2 + 1) * 512], AF.Silu, bias=bgate_col[:, m, ee:ee + 1]))(mb, e, h2),
                                             deps=dep_ge, tag=f"t{t}s{sub}e{e}gev{mb}_{h2}")
                        mms_u = []
                        for db in range(DB):
                            dep = []
                            if db == 0:
                                dep = [ld_u, hT_deps[h2]]
                                dep.append(hid_h[h2][mb - 2] if mb >= 2 else (pfr[1] if h2 == 0 else hid_h[0][MB - 2 + mb]))
                            mms_u.append(S.op("tensor",
                                              MM(uq, hs_col(k_u, db * 1024 + mcol0, 128), hT[:, db, h2 * 512:(h2 + 1) * 512], db == 0, db == DB - 1),
                                              deps=dep, tag=f"t{t}s{sub}e{e}u{mb}"))
                        if h2 == 1:
                            mark_half_use(k_u, mms_u[-1])
                        if h2 == 1 and mb == 15:
                            ld_o[1], k_o[1] = load_half("out", e, 1)
                        hid_h[h2][mb] = S.op("vector",
                                             (lambda m, ee, h_: lambda v: v.scalar_tensor_tensor(hid[:, m, h_ * 512:(h_ + 1) * 512], PF1[:, (m % 2) * 512:(m % 2 + 1) * 512], bup_col[:, m, ee:ee + 1], gtmp[:, m % 4, h_ * 512:(h_ + 1) * 512], ALU.add, ALU.mult))(mb, e, h2),
                                             deps=[mms_u[-1], ev_gh[h2][mb]], tag=f"t{t}s{sub}e{e}h{mb}_{h2}")
                hid_hist = hid_h[1][:]
                hid_hist[0:0] = []
                for mb in range(MB):
                    hid_hist[mb] = hid_h[1][mb]
                ev_g = ev_gh[1]
            else:
              for mb in range(MB):
                k_g = k_gA if mb < 8 else k_gB
                k_u = k_uA if mb < 8 else k_uB
                ld_g = ld_gA if mb < 8 else ld_gB
                ld_u = ld_uA if mb < 8 else ld_uB
                mcol0 = (mb % 8) * 128
                mms = []
                for h2 in range(2):
                    for db in range(DB):
                        dep = []
                        if h2 == 0 and db == 0:
                            dep = [hT_deps[0], ld_g]
                            dep.append(ev_g[mb - 1] if mb >= 1 else pfr[0])
                        mms.append(S.op("tensor",
                                        MM(PF0[:, h2 * 512:(h2 + 1) * 512], hs_col(k_g, db * 1024 + mcol0, 128), hT[:, db, h2 * 512:(h2 + 1) * 512], db == 0, db == DB - 1),
                                        deps=dep, tag=f"t{t}s{sub}e{e}g{mb}"))
                g_last = mms[-1]
                mark_half_use(k_g, g_last)
                if mb == 7:
                    ld_o[0], k_o[0] = load_half("out", e, 0)
                dep_ge = [g_last, ev_g[mb - 2] if mb >= 2 else None,
                          hid_hist[mb - 4] if mb >= 4 else prev_hid[MB - 4 + mb]]
                ev_g[mb] = S.op("scalar",
                                (lambda m, ee: lambda s: s.activation(gtmp[:, m % 4, :], PF0[:], AF.Silu, bias=bgate_col[:, m, ee:ee + 1]))(mb, e),
                                deps=dep_ge, tag=f"t{t}s{sub}e{e}gev{mb}")
                mms_u = []
                for h2 in range(2):
                    for db in range(DB):
                        dep = []
                        if h2 == 0 and db == 0:
                            dep = [ld_u, hT_deps[0]]
                            dep.append(hid_hist[mb - 1] if mb >= 1 else pfr[1])
                        mms_u.append(S.op("tensor",
                                          MM(PF1[:, h2 * 512:(h2 + 1) * 512], hs_col(k_u, db * 1024 + mcol0, 128), hT[:, db, h2 * 512:(h2 + 1) * 512], db == 0, db == DB - 1),
                                          deps=dep, tag=f"t{t}s{sub}e{e}u{mb}"))
                u_last = mms_u[-1]
                mark_half_use(k_u, u_last)
                if mb == 15:
                    ld_o[1], k_o[1] = load_half("out", e, 1)
                hid_hist[mb] = S.op("vector",
                                    (lambda m, ee: lambda v: v.scalar_tensor_tensor(hid[:, m, :], PF1[:], bup_col[:, m, ee:ee + 1], gtmp[:, m % 4, :], ALU.add, ALU.mult))(mb, e),
                                    deps=[u_last, ev_g[mb]], tag=f"t{t}s{sub}e{e}h{mb}")
            quarters = [PF0[:, 0:512], PF0[:, 512:1024], PF1[:, 0:512], PF1[:, 512:1024]]
            q_free = [ev_g[MB - 1], ev_g[MB - 1], hid_hist[MB - 1], hid_hist[MB - 1]]
            down_last = [None, None]
            for h2 in range(2):
                for tb in range(NBLK):
                    j = (h2 * NBLK + tb) % 4
                    mms_o = []
                    first_q = (h2 == 0 and tb < 4)
                    for mb in range(MB):
                        dep = []
                        if mb == 0:
                            dep = [ld_o[h2], q_free[j]] + (hid_hist if first_q else [])
                        mms_o.append(S.op("tensor",
                                          MM(quarters[j], hid[:, mb, tb * 128:(tb + 1) * 128], hs_col(k_o[h2], mb * 512, 512), mb == 0, mb == MB - 1),
                                          deps=dep, tag=f"t{t}s{sub}e{e}o{tb}_{h2}"))
                    down_last[h2] = mms_o[-1]
                    stt = S.op("vector",
                               (lambda b, jj, hh, ee: lambda v: v.scalar_tensor_tensor(x_sb[:, b, hh * 512:(hh + 1) * 512], quarters[jj], w_half[:, b, ee:ee + 1], x_sb[:, b, hh * 512:(hh + 1) * 512], ALU.mult, ALU.add))(tb, j, h2, e),
                               deps=[mms_o[-1], w_ops[tb], last_stt_tb[tb]], tag=f"t{t}s{sub}e{e}stt{tb}_{h2}")
                    stt_ops.append(stt)
                    last_stt_tb[tb] = stt
                    q_free[j] = stt
                    if e == NE - 1 and h2 == 1:
                        last_tb_final[tb] = stt
                mark_half_use(k_o[h2], down_last[h2])
            prev_hid = hid_hist
            pfr = [q_free[1], q_free[3]]
        if f"hid@{t}_{sub}" in taps:
            tap(f"hid@{t}_{sub}", hid[:], hid_hist)
        tap(f"x_mid@{t}_{sub}", x_sb[:], last_tb_final)
        return last_tb_final

    def attn_phase(t, hT_deps, attn_sub=2):
        ld_q, k_q = load_half("attn", 0)
        ld_k, k_k = load_half("attn", 1)
        ld_v, k_v = load_half("attn", 2)
        ld_p, k_p = load_half("attn", 3)
        pf = [PF0, PF1]
        phi_ops = []
        for which, dst in ((0, phiq), (1, phik)):
            k_w = k_q if which == 0 else k_k
            ld_w = ld_q if which == 0 else ld_k
            for ib in range(DB):
                par = (which * DB + ib) % 2
                mms = []
                for h2 in range(2):
                    for db in range(DB):
                        dep = []
                        if h2 == 0 and db == 0:
                            dep = [hT_deps[0], ld_w]
                            if len(phi_ops) >= 2:
                                dep.append(phi_ops[-2])
                        elif h2 == 1 and db == 0 and ib == 0 and which == 0:
                            dep = [hT_deps[1]]
                        mms.append(S.op("tensor",
                                        MM(pf[par][:, h2 * 512:(h2 + 1) * 512], hs_col(k_w, db * 1024 + ib * 128, 128), hT[:, db, h2 * 512:(h2 + 1) * 512], db == 0, db == DB - 1),
                                        deps=dep, tag=f"t{t}a_qk{which}_{ib}"))
                if ib == DB - 1:
                    mark_half_use(k_w, mms[-1])
                mn = S.op("vector", (lambda p_: lambda v: v.tensor_scalar_min(tq[:], pf[p_][:], 0.0))(par),
                          deps=[mms[-1]] + phi_ops[-1:], tag=f"t{t}a_min{which}_{ib}")
                exp_ = S.op("scalar", lambda s: s.activation(e_t[:], tq[:], AF.Exp), deps=[mn], tag=f"t{t}a_exp{which}_{ib}")
                phi_ops.append(S.op("vector",
                                    (lambda d_, i_, p_: lambda v: v.scalar_tensor_tensor(d_[:, i_, :], pf[p_][:], 0.0, e_t[:], ALU.max, ALU.add))(dst, ib, par),
                                    deps=[exp_], tag=f"t{t}a_phi{which}_{ib}"))
        v_evs = []
        v_last_mm = None
        for tb in range(NBLK):
            par = tb % 2
            mms = []
            for h2 in range(2):
                for db in range(DB):
                    dep = []
                    if h2 == 0 and db == 0:
                        dep = [ld_v, phi_ops[-2 + par], hT_deps[0 if tb < 4 else 1]]
                        if len(v_evs) >= 2:
                            dep.append(v_evs[-2])
                    mms.append(S.op("tensor",
                                    MM(pf[par][:, h2 * 512:(h2 + 1) * 512], hT[:, db, tb * 128:(tb + 1) * 128], hs_col(k_v, db * 1024 + h2 * 512, 512), db == 0, db == DB - 1),
                                    deps=dep, tag=f"t{t}a_v{tb}"))
            v_last_mm = mms[-1]
            src = bass.AP(pf[par], 0, [[1024, 128], [64, H], [1, 64]])
            v_evs.append(S.op("vector",
                              (lambda b, s_: lambda v: v.tensor_copy(v_tm[:, b, :, 0:64], s_))(tb, src),
                              deps=[mms[-1]], tag=f"t{t}a_vev{tb}"))
        mark_half_use(k_v, v_last_mm)
        ms_vones = S.op("vector", lambda v: v.memset(v_tm[:, :, :, 64:65], 1.0), deps=v_evs, tag=f"t{t}a_vones")
        if attn_sub == 0:
            mark_slot_use(si_vp, v_last_mm)
            return [ms_vones]
        tap(f"phiq@{t}_2", phiq[:], phi_ops)
        tap(f"phik@{t}_2", phik[:], phi_ops)
        tap(f"v_tm@{t}_2", v_tm[:], [ms_vones])
        bsc = S.barrier(f"t{t}_scanbar", extra=[st["last_tap"]])
        prev_xadd = None
        prev_A_ev = [None, None]
        prev_N_read = None
        prev_tk_ev = None
        prev_ty_ev = None
        prev_am_read = [None, None]
        prev_ktm_read = [None]
        prev_ytok_read = [None, None]
        prev_kz_read = [None] * GH
        prev_pproj = None
        xadds = []
        pp_S = S_sb.shape[1] * S_sb.shape[2]
        for c in range(NBLK):
            zred = S.op("vector",
                        (lambda cc: lambda v: v.reduce_sum(zz_col[:], phik[:, :, cc * 128:(cc + 1) * 128], axis=AX.X))(c),
                        deps=[phi_ops[-1], st["zcopy"]], tag=f"t{t}c{c}_zred")
            sc_deps = []
            all_ytr = []
            tk_trs = []
            for pb2 in range(NPAIR):
                dep = []
                if pb2 == 0:
                    dep = [phi_ops[-1], prev_tk_ev]
                tk_trs.append(S.op("tensor",
                                   (lambda p_b, cc: lambda te: te.transpose(PB1[:, p_b, :], phik[:, p_b, cc * 128:(cc + 1) * 128], ident[:]))(pb2, c),
                                   deps=dep, tag=f"t{t}c{c}ktr{pb2}"))
            tk_ev = S.op("vector",
                         lambda v: v.tensor_copy(phik_tm[:], PB1[:]),
                         deps=tk_trs + [prev_ktm_read[0]], tag=f"t{t}c{c}kev")
            prev_tk_ev = tk_ev
            for g in range(NG):
                par = g % 2
                psA = bass.AP(PF1, par * 512, [[1024, 128], [128, GH], [1, 128]])
                kz = []
                for i in range(GH):
                    h = g * GH + i
                    pb, pbase = h // 2, (h % 2) * 64
                    kz.append(S.op("vector",
                                   (lambda p_b, p_base, ii, cc: lambda v: v.tensor_copy(phik_z[p_base:p_base + 64, ii, :], phik[p_base:p_base + 64, p_b, cc * 128:(cc + 1) * 128]))(pb, pbase, i, c),
                                   deps=[phi_ops[-1], prev_kz_read[i]], tag=f"t{t}c{c}g{g}kz{i}"))
                amms = []
                for i in range(GH):
                    h = g * GH + i
                    pb = h // 2
                    dep = [kz[i]] + ([prev_A_ev[par]] if i == 0 else [])
                    amms.append(S.op("tensor",
                                     (lambda p_b, ii, cc, pA: lambda te: te.matmul(pA[:, ii, :], phik_z[:, ii, :], phiq[:, p_b, cc * 128:(cc + 1) * 128], start=True, stop=True))(pb, i, c, psA),
                                     deps=dep, tag=f"t{t}c{c}g{g}A{i}"))
                for i in range(GH):
                    prev_kz_read[i] = amms[i]
                mbc = bass.AP(trimask, 0, [[128, 128], [0, GH], [1, 128]])
                a_ev = S.op("vector",
                            (lambda p_, pA: lambda v: v.tensor_tensor(a_m[:, p_, :, :], pA, mbc, ALU.mult))(par, psA),
                            deps=[amms[-1], prev_am_read[par]], tag=f"t{t}c{c}g{g}Aev")
                prev_A_ev[par] = a_ev
                nmms = []
                for i in range(GH):
                    h = g * GH + i
                    pb = h // 2
                    dep = []
                    if i == 0:
                        dep = [a_ev, ms_vones, prev_N_read, st["scopy"], st["zcopy"]]
                    nmms.append(S.op("tensor",
                                     (lambda hh, ii, cc, p_: lambda te: te.matmul(PF2[:, ii, 0:65], a_m[:, p_, ii, :], v_tm[:, cc, hh, :], start=True, stop=False))(h, i, c, par),
                                     deps=dep, tag=f"t{t}c{c}g{g}n1_{i}"))
                    nmms.append(S.op("tensor",
                                     (lambda hh, p_b, ii, cc: lambda te: te.matmul(PF2[:, ii, 0:65], phiq[:, p_b, cc * 128:(cc + 1) * 128], S_sb[:, hh, :], start=False, stop=True))(h, pb, i, c),
                                     deps=[], tag=f"t{t}c{c}g{g}n2_{i}"))
                prev_am_read[par] = nmms[-2]
                smms = []
                for i in range(GH):
                    h = g * GH + i
                    pb, pbase = h // 2, (h % 2) * 64
                    dep = [tk_ev, st["scopy"]] if i == 0 else []
                    first = st["first_supd_par"][h % 2]
                    st["first_supd_par"][h % 2] = False
                    smms.append(S.op("tensor",
                                     (lambda hh, p_b, p_base, fst, cc: lambda te: te.matmul(psS[p_base:p_base + 64, p_b, :], phik_tm[:, p_b, p_base:p_base + 64], v_tm[:, cc, hh, 0:64], start=fst, stop=False, skip_group_check=True))(h, pb, pbase, first, c),
                                     deps=dep, tag=f"t{t}c{c}g{g}S{i}"))
                prev_ktm_read[0] = smms[-1]
                sc_deps.append(smms[-1])
                den_src = bass.AP(PF2, 64, [[GH * 128, 128], [128, GH]])
                rd1 = S.op("vector", lambda v: v.tensor_scalar_add(rd_s[:], den_src, ATT_EPS),
                           deps=[nmms[-1], prev_N_read], tag=f"t{t}c{c}g{g}rd1")
                S.op("vector", lambda v: v.drain(), tag="dr")
                rd2 = S.op("vector", lambda v: v.reciprocal(rd_s[:], rd_s[:]), deps=[rd1], tag=f"t{t}c{c}g{g}rd2")
                yss = []
                for i in range(GH):
                    h = g * GH + i
                    pl, pbase = (h % 4) // 2, (h % 2) * 64
                    yss.append(S.op("vector",
                                    (lambda ii, p_, p_l, p_base: lambda v: v.tensor_scalar(y_tok[:, p_, p_l, p_base:p_base + 64], PF2[:, ii, 0:64], rd_s[:, ii:ii + 1], None, ALU.mult))(i, par, pl, pbase),
                                    deps=[rd2, prev_ytok_read[par] if i == 0 else None], tag=f"t{t}c{c}g{g}ys{i}"))
                prev_N_read = yss[-1]
                st["pf2_free"] = yss[-1]
                ytr = []
                for j in range(2):
                    dep = [yss[2 * j], yss[2 * j + 1]]
                    if g == 0 and j == 0:
                        dep.append(prev_ty_ev)
                    ytr.append(S.op("tensor",
                                    (lambda gg, jj, p_: lambda te: te.transpose(PB0[:, gg * 2 + jj, :], y_tok[:, p_, jj, :], ident[:]))(g, j, par),
                                    deps=dep, tag=f"t{t}c{c}g{g}ytr{j}"))
                prev_ytok_read[par] = ytr[-1]
                if g == NG - 1:
                    ty_ev = S.op("vector",
                                 lambda v: v.tensor_copy(yT_s[:], PB0[:]),
                                 deps=ytr + all_ytr + [prev_pproj], tag=f"t{t}c{c}ytev")
                    prev_ty_ev = ty_ev
                    last_ytev = ty_ev
                    st["pb_free"] = ty_ev
                all_ytr.extend(ytr)
            sc_e = S.op("vector",
                        lambda v: v.tensor_copy(bass.AP(S_sb, 0, [[pp_S, 64], [2 * 65, NPAIR], [1, 64]]), psS[0:64, :, :]),
                        deps=sc_deps, tag=f"t{t}c{c}_scopyE")
            sc_o = S.op("vector",
                        lambda v: v.tensor_copy(bass.AP(S_sb, 64 * pp_S + 65, [[pp_S, 64], [2 * 65, NPAIR], [1, 64]]), psS[64:128, :, :]),
                        deps=sc_deps, tag=f"t{t}c{c}_scopyO")
            za = S.op("vector", lambda v: v.tensor_add(z_f32[:], z_f32[:], zz_col[:]), deps=[zred], tag=f"t{t}c{c}_zadd")
            S.op("vector", lambda v: v.drain(), tag="dr")
            zc_e = S.op("vector",
                        lambda v: v.tensor_copy(bass.AP(S_sb, 64, [[pp_S, 64], [2 * 65, NPAIR], [1, 1]]),
                                                bass.AP(z_f32, 0, [[NPAIR, 64], [1, NPAIR], [1, 1]])),
                        deps=[za], tag=f"t{t}c{c}_zcE")
            zc_o = S.op("vector",
                        lambda v: v.tensor_copy(bass.AP(S_sb, 64 * pp_S + 65 + 64, [[pp_S, 64], [2 * 65, NPAIR], [1, 1]]),
                                                bass.AP(z_f32, 64 * NPAIR, [[NPAIR, 64], [1, NPAIR], [1, 1]])),
                        deps=[za], tag=f"t{t}c{c}_zcO")
            st["scopy"] = sc_o
            st["zcopy"] = zc_o
            st["scopy_e"] = sc_e
            st["zcopy_e"] = zc_e
            if attn_sub == 1:
                prev_xadd = st["zcopy"]
                continue
            pms = []
            for h2 in range(2):
                for pb in range(NPAIR):
                    dep = []
                    if h2 == 0 and pb == 0:
                        dep = [last_ytev, ld_p, prev_xadd if prev_xadd is not None else phi_ops[-2]]
                    pms.append(S.op("tensor",
                                    MM(PF0[:, h2 * 512:(h2 + 1) * 512], yT_s[:, pb, :], hs_col(k_p, pb * 1024 + h2 * 512, 512), pb == 0, False),
                                    deps=dep, tag=f"t{t}c{c}p{pb}"))
                pms.append(S.op("tensor",
                                MM(PF0[:, h2 * 512:(h2 + 1) * 512], ones_r[0:1, :], bp_row[0:1, h2 * 512:(h2 + 1) * 512], False, True),
                                deps=[], tag=f"t{t}c{c}pb"))
            prev_pproj = pms[-1]
            prev_xadd = S.op("vector",
                             (lambda cc: lambda v: v.tensor_add(x_sb[:, cc, :], x_sb[:, cc, :], PF0[:]))(c),
                             deps=[pms[-1]], tag=f"t{t}c{c}_xadd")
            xadds.append(prev_xadd)
        mark_half_use(k_p, prev_pproj)
        st["xadds"] = xadds
        tap(f"S_sb@{t}_2", S_sb[:], [st["scopy"], st["zcopy"]])
        tap(f"yT_s@{t}_2", yT_s[:], [prev_xadd])
        return xadds if len(xadds) == NBLK else [prev_xadd]

    prev_store = [None] * NBLK
    pending_loads = None
    for t in range(n_tiles):
        if pending_loads is not None:
            x_ld = pending_loads
        else:
            x_ld = []
            for pr in range(NBLK // 2):
                b0 = pr * 2
                x_src = x_in.rearrange("(n p) d -> p n d", p=128)[:, t * NBLK + b0:t * NBLK + b0 + 2, :]
                ldp = S.op("sync",
                           (lambda xs, b: lambda s: s.dma_start(x_sb[:, b:b + 2, :], xs))(x_src, b0),
                           deps=list(setup_done), dma_sem=f"q_l{pr}", tag=f"t{t}_ldx{pr}")
                x_ld.extend([ldp, ldp])
        ld_x = x_ld[-1]

        cur = x_ld
        x_prev_tb = x_ld
        if level >= 1:
            hT_ev, apps1 = ln_phase(t, 1, cur)
            cur = hT_ev
        if level >= 2:
            w_ops, rev1, rev3 = router_phase(t, 1, hT_ev, apps1, x_prev_tb)
            cur = w_ops
        if level >= 3:
            ffn_last = ffn_phase(t, 1, hT_ev, w_ops, rev1, rev3, apps1, x_prev_tb)
            cur = ffn_last
            x_prev_tb = ffn_last
        if level >= 4:
            hT_ev2, apps2 = ln_phase(t, 2, cur)
            cur = hT_ev2
        if level >= 5:
            attn_last = attn_phase(t, hT_ev2, attn_sub=min(2, level - 5))
            cur = attn_last
            if len(attn_last) == NBLK:
                x_prev_tb = attn_last
        if level >= 8:
            hT_ev3, apps3 = ln_phase(t, 3, cur)
            w_ops3, rev1b, rev3b = router_phase(t, 3, hT_ev3, apps3, x_prev_tb)
            ffn3_last = ffn_phase(t, 3, hT_ev3, w_ops3, rev1b, rev3b, apps3, x_prev_tb)
            cur = ffn3_last
        ffn3_last = cur

        pending_loads = [] if t + 1 < n_tiles else None
        for pr in range(NBLK // 2):
            b0 = pr * 2
            y_dst = y_out.rearrange("(n p) d -> p n d", p=128)[:, t * NBLK + b0:t * NBLK + b0 + 2, :]
            dep = [ffn3_last[b0], ffn3_last[b0 + 1]] if len(ffn3_last) == NBLK else [ffn3_last[-1]]
            stp = S.op("gpsimd",
                       (lambda yd, b: lambda g: g.dma_start(yd, x_sb[:, b:b + 2, :]))(y_dst, b0),
                       deps=dep, dma_sem=f"q_x{pr}", tag=f"t{t}_stx{pr}")
            prev_store[b0] = prev_store[b0 + 1] = stp
            if pending_loads is not None:
                x_src = x_in.rearrange("(n p) d -> p n d", p=128)[:, (t + 1) * NBLK + b0:(t + 1) * NBLK + b0 + 2, :]
                ldp = S.op("sync",
                           (lambda xs, b: lambda s: s.dma_start(x_sb[:, b:b + 2, :], xs))(x_src, b0),
                           deps=[stp], dma_sem=f"q_l{pr}", tag=f"t{t+1}_ldx{pr}")
                pending_loads.extend([ldp, ldp])
        st["x_store"] = prev_store[-1]

    last_dma = {}
    for eng in ENGINES:
        for o in S.ops[eng]:
            if o.dma_sem:
                last_dma[o.dma_sem] = (eng, o)
    for semname, (eng, o) in sorted(last_dma.items()):
        S.op(eng, lambda g: None, deps=[o, st["x_store"]], tag=f"tail_{semname}")

    S.finalize()
    sems = {name: es.enter_context(nc.semaphore(name)) for name in S.sem_names}
    block = es.enter_context(nc.Block())
    with es:
        block.tensor(lambda t: S.emit("tensor", t, sems))
        block.scalar(lambda s: S.emit("scalar", s, sems))
        block.vector(lambda v: S.emit("vector", v, sems))
        block.gpsimd(lambda g: S.emit("gpsimd", g, sems))
        block.sync(lambda sy: S.emit("sync", sy, sems))
    return nc


_NC_CACHE = {}


def _get_program():
    if "nc" not in _NC_CACHE:
        _NC_CACHE["nc"] = build_program()
    return _NC_CACHE["nc"]


def kernel(**inputs):
    x = np.ascontiguousarray(np.asarray(inputs["L0_attn_x"], dtype=np.float32))
    ident = np.eye(128).astype(ml_dtypes.bfloat16)
    trimask = np.triu(np.ones((128, 128))).astype(ml_dtypes.bfloat16)
    common = {
        "wq": np.ascontiguousarray(np.asarray(inputs["L0_attn_Wq"], dtype=np.float32)),
        "wk": np.ascontiguousarray(np.asarray(inputs["L0_attn_Wk"], dtype=np.float32)),
        "wv": np.ascontiguousarray(np.asarray(inputs["L0_attn_Wv"], dtype=np.float32)),
        "wp": np.ascontiguousarray(np.asarray(inputs["L0_attn_Wp"], dtype=np.float32)),
        "bp": np.ascontiguousarray(np.asarray(inputs["L0_attn_bp"], dtype=np.float32)),
        "wrouter": np.ascontiguousarray(np.asarray(inputs["L0_ffn_W_router"], dtype=np.float32)),
        "wgate": np.ascontiguousarray(np.asarray(inputs["L0_ffn_W_gate"], dtype=np.float32)),
        "wup": np.ascontiguousarray(np.asarray(inputs["L0_ffn_W_up"], dtype=np.float32)),
        "wout": np.ascontiguousarray(np.asarray(inputs["L0_ffn_W_out"], dtype=np.float32)),
        "bgate": np.ascontiguousarray(np.asarray(inputs["L0_ffn_b_gate"], dtype=np.float32)),
        "bup": np.ascontiguousarray(np.asarray(inputs["L0_ffn_b_up"], dtype=np.float32)),
        "bout": np.ascontiguousarray(np.asarray(inputs["L0_ffn_b_out"], dtype=np.float32)),
        "ident": ident,
        "trimask": trimask,
    }
    nc = _get_program()
    in_maps = [dict(common, x=np.ascontiguousarray(x[c])) for c in range(8)]
    res = run_bass_kernel_spmd(nc, in_maps, core_ids=list(range(8)))
    return np.stack([np.asarray(res.results[c]["y"], dtype=np.float32) for c in range(8)], axis=0)


if __name__ == "__main__":
    nc = build_program()
    n_ops = sum(1 for _ in ())
    print("built ok")
